# version 23
# baseline (speedup 1.0000x reference)
import numpy as np
from contextlib import ExitStack

import concourse.bass as bass
import concourse.mybir as mybir
from concourse.bass_utils import run_bass_kernel_spmd

F32 = mybir.dt.float32
BF16 = mybir.dt.bfloat16
AF = mybir.ActivationFunctionType
ALU = mybir.AluOpType
AX = mybir.AxisListType

B, T, D = 2, 16384, 1024
HF, R = 4096, 16
NH, DH, SINKS, WIN, CTX = 16, 64, 4, 512, 516
CPB = 4
ACT_TOT = 516
ACT0 = T - ACT_TOT
NACT = ACT_TOT // CPB
NIN = ACT0 // CPB
TOK = NIN + NACT
EPS = 1e-5
NEG = -1.0e9

TT = 256
NTILE = TOK // TT
DCN = D // 128
HCN = HF // 128
OCN = D // 128
FCP = 128
PJP = 32
NFC = HF // FCP
NPJ = D // PJP

ENGS = ("tensor", "scalar", "vector", "gpsimd", "sync")


class Em:

    def __init__(self):
        self.q = {e: [] for e in ENGS}
        self.cnt = {}
        self.guards = {}

    def sem(self, name):
        self.cnt[name] = 0

    def inc(self, name, n=1):
        self.cnt[name] += n
        return self.cnt[name]

    def wait(self, eng, sem, val):
        if val > self.cnt[sem]:
            raise RuntimeError(f"wait {sem}>={val}, emitted {self.cnt[sem]}")
        if val > 0:
            self.q[eng].append(("w", sem, val))

    ENG_SEM = {"vector": "sV", "scalar": "sA", "tensor": "sP",
               "gpsimd": "sG"}

    def drain(self, eng):
        sem = self.ENG_SEM.get(eng)
        if sem is not None and sem in self.cnt:
            for idx in range(len(self.q[eng]) - 1, -1, -1):
                a = self.q[eng][idx]
                if a[0] == "o":
                    if a[4] is None:
                        self.inc(sem)
                        self.q[eng][idx] = (a[0], a[1], a[2], a[3], (sem, 1))
                        self.q[eng].append(("w", sem, self.cnt[sem]))
                        return
                    if a[4][0] == sem:
                        self.q[eng].append(("w", sem, self.cnt[sem]))
                        return
                    break
                if a[0] in ("w", "d"):
                    continue
            self.q[eng].append(("d",))
        else:
            self.q[eng].append(("d",))

    def op(self, eng, fname, *args, inc=None, **kwargs):
        if inc is not None and not isinstance(inc, tuple):
            inc = (inc, 1)
        if inc is not None:
            self.inc(inc[0], inc[1])
        self.q[eng].append(("o", fname, args, kwargs, inc))
        return self.cnt[inc[0]] if inc else None

    def dma(self, eng, out, in_, sem, **kwargs):
        return self.op(eng, "dma_start", inc=(sem, 16), out=out, in_=in_, **kwargs)

    def gwait(self, eng, key):
        for sem, val in self.guards.pop(key, []):
            self.wait(eng, sem, val)

    def gwait2(self, engs, key):
        for sem, val in self.guards.pop(key, []):
            for e in engs:
                self.wait(e, sem, val)

    def gset(self, key, sem):
        self.guards.setdefault(key, []).append((sem, self.cnt[sem]))

    def replay(self, eng_name, engobj, sems):
        for act in self.q[eng_name]:
            if act[0] == "w":
                engobj.wait_ge(sems[act[1]], act[2])
            elif act[0] == "d":
                engobj.drain()
            else:
                _, fname, args, kwargs, incv = act
                inst = getattr(engobj, fname)(*args, **kwargs)
                if incv is not None:
                    inst.then_inc(sems[incv[0]], incv[1])


def build_nc(n_tiles=NTILE):
    nc = bass.Bass("TRN2", target_bir_lowering=False)

    DI = lambda n, s: nc.dram_tensor(n, s, F32, kind="ExternalInput")
    t_x = DI("xT", [D, TOK])
    t_ctx = DI("ctxT", [D, CTX])
    t_mwA = DI("mwA", [128, WIN])
    t_mwB = DI("mwB", [1, WIN])
    t_msA = DI("msA", [128, SINKS])
    t_msB = DI("msB", [1, SINKS])
    t_cvec = DI("cvec", [128, 8])
    t_crow = DI("crow", [1, 128])
    t_id = DI("ident", [128, 128])
    t_wq = DI("wq", [2 * 128, DCN * 512])
    t_wk = DI("wk", [2 * 128, DCN * 512])
    t_wv = DI("wv", [2 * 128, DCN * 512])
    t_wo = DI("wo", [2 * 128, DCN * 512])
    t_wfc = DI("wfc_st", [NFC * 128, DCN * FCP])
    t_wpj = DI("wpj_st", [NPJ * 128, HCN * PJP])
    t_uT = DI("uT", [R, HF])
    t_vT = DI("vT", [R, D])
    t_fbT = DI("fbT", [R, R])
    t_gate = DI("gate", [1, 1])
    t_out = nc.dram_tensor("outT", [D, TOK], F32, kind="ExternalOutput")

    E = Em()
    for s in ("dIn", "dS0", "dS1", "dO0", "dO1", "dGi", "dM", "dC", "dW",
              "sP", "sA", "sV", "sG"):
        E.sem(s)

    with ExitStack() as octx:
        sbuf = lambda n, s, dt: octx.enter_context(nc.sbuf_tensor(n, s, dt))
        s_fc = sbuf("Sfc", [128, DCN * HF], BF16)
        s_pj = sbuf("Spj", [128, HCN * D], BF16)
        stg = sbuf("stg", [128, 2048], F32)
        sgn1 = sbuf("sgn1", [128, 2048], BF16)
        u2T = sbuf("u2T", [16, HF], BF16)
        srow = sbuf("srow", [1, 768], F32)
        srowp = sbuf("srowp", [1, 384], F32)
        hsc2 = sbuf("hsc2", [128, NFC], F32)
        osch = sbuf("osch", [128, OCN], F32)
        rbb = sbuf("rbb", [128, 256], F32)
        fbTf = sbuf("fbTf", [16, R], F32)
        fbTg = sbuf("fbTg", [16, R], BF16)
        gatef = sbuf("gatef", [1, 1], F32)
        g2c = sbuf("g2c", [16, 1], F32)
        lnscr = sbuf("lnscr", [128, 4096], BF16)
        zrow = sbuf("zrow", [1, 256], BF16)
        cvec = sbuf("cvecs", [128, 8], F32)
        cvecb = sbuf("cvecb", [128, 8], BF16)
        crow = sbuf("crows", [1, 128], F32)
        identb = sbuf("identb", [128, 128], BF16)
        xact = sbuf("xact", [128, DCN * NACT], F32)
        nmb = sbuf("nmb", [128, 2], F32)

        pbk = [octx.enter_context(nc.psum_tensor(f"pb{i}", [128, 512], F32))
               for i in range(7)]
        pt = octx.enter_context(nc.psum_tensor("pt", [128, 1024], BF16))
        ST = pbk[6]
        BC = pbk[2]

        ones_b = cvecb[:, 0:1]
        negones_b = cvecb[:, 5:6]
        eps_p1 = cvec[0:1, 3:4]
        ptf = pt[:, 0:1024].bitcast(F32)
        idf1 = cvec[0:1, 0:1]
        v3 = lambda t, w: t[:].rearrange("p (c t) -> p c t", t=w)
        lnscr3 = v3(lnscr, 512)
        sfc3 = v3(s_fc, HF)
        spj3 = v3(s_pj, D)
        stg3a = stg[:, 0:1024].rearrange("p (c t) -> p c t", c=DCN)
        stg3b = stg[:, 1024:2048].rearrange("p (c t) -> p c t", c=DCN)

        def emit_ln_gen(src3, dst3, W, src_wait=(), precast=False, scr=None,
                        scr_key="lnscr", dst_key=None, fscr3=None,
                        cast_split=False, stat_waits=(), out=None,
                        t3_key=None):
            scr = lnscr if scr is None else scr
            scr3 = v3(scr, 512)
            cast_v = None
            if not precast:
                for sem, val in src_wait:
                    E.wait("scalar", sem, val)
                if cast_split:
                    E.gwait2(("scalar", "vector"), scr_key)
                    E.op("scalar", "activation", scr3[:, 0:4, 0:W],
                         src3[:, 0:4, :], AF.Copy)
                    E.op("scalar", "activation", scr3[:, 0:4, 256:256 + W],
                         src3[:, 0:4, :], AF.Square, inc="sA")
                    for sem, val in src_wait:
                        E.wait("vector", sem, val)
                    E.op("vector", "tensor_copy", scr3[:, 4:8, 0:W],
                         src3[:, 4:8, :])
                    E.op("vector", "tensor_tensor", scr3[:, 4:8, 256:256 + W],
                         src3[:, 4:8, :], src3[:, 4:8, :], ALU.mult,
                         inc="sV")
                    cast_v = E.cnt["sV"]
                else:
                    E.gwait("scalar", scr_key)
                    E.op("scalar", "activation", scr3[:, :, 0:W], src3,
                         AF.Copy)
                    E.op("scalar", "activation", scr3[:, :, 256:256 + W],
                         src3, AF.Square, inc="sA")
            cA = E.cnt["sA"]
            yield "casts"
            E.gwait("tensor", "ST")
            E.wait("tensor", "sA", cA)
            if cast_v is not None:
                E.wait("tensor", "sV", cast_v)
            for sem, val in stat_waits:
                E.wait("tensor", sem, val)
            for dc in range(DCN):
                E.op("tensor", "matmul", ST[0:1, 0:512], ones_b,
                     scr[:, dc * 512: dc * 512 + 512],
                     start=(dc == 0), stop=(dc == DCN - 1),
                     inc=("sP" if dc == DCN - 1 else None))
            cP = E.cnt["sP"]
            E.gset(scr_key, "sP")
            yield "stats"
            E.gwait("vector", "srow")
            E.wait("vector", "sP", cP)
            E.op("vector", "tensor_scalar", srow[0:1, 0:W], ST[0:1, 0:W],
                 1.0 / D, None, ALU.mult)
            E.op("vector", "tensor_scalar", srow[0:1, 256:256 + W],
                 ST[0:1, 256:256 + W], 1.0 / D, None, ALU.mult, inc="sV")
            E.gset("ST", "sV")
            E.drain("vector")
            E.op("vector", "tensor_mul", srow[0:1, 512:512 + W],
                 srow[0:1, 0:W], srow[0:1, 0:W])
            E.drain("vector")
            E.op("vector", "tensor_sub", srow[0:1, 256:256 + W],
                 srow[0:1, 256:256 + W], srow[0:1, 512:512 + W], inc="sV")
            cV1 = E.cnt["sV"]
            E.wait("scalar", "sV", cV1)
            E.op("scalar", "activation", srow[0:1, 512:512 + W],
                 srow[0:1, 256:256 + W], AF.Ln, bias=eps_p1)
            E.drain("scalar")
            E.op("scalar", "activation", srow[0:1, 256:256 + W],
                 srow[0:1, 512:512 + W], AF.Exp, scale=-0.5, inc="sA")
            cA2 = E.cnt["sA"]
            E.wait("vector", "sA", cA2)
            E.op("vector", "tensor_mul", srow[0:1, 512:512 + W],
                 srow[0:1, 0:W], srow[0:1, 256:256 + W], inc="sV")
            cV2 = E.cnt["sV"]
            yield "rows"
            E.wait("tensor", "sV", cV2)
            E.op("tensor", "matmul", ST[:, 0:W], crow[:],
                 srow[0:1, 256:256 + W], start=True, stop=True)
            E.op("tensor", "matmul", ST[:, 256:256 + W], crow[:],
                 srow[0:1, 512:512 + W], start=True, stop=True, inc="sP")
            cPb = E.cnt["sP"]
            E.gset("srow", "sP")
            yield "bcast"
            E.wait("vector", "sP", cPb)
            if dst_key is not None:
                E.gwait("vector", dst_key)
            if t3_key is not None:
                E.gwait("vector", t3_key)
            rb = ST[:, 0:W].unsqueeze(1).broadcast_to((128, DCN, W))
            mrb = ST[:, 256:256 + W].unsqueeze(1).broadcast_to((128, DCN, W))
            t3 = dst3 if fscr3 is None else fscr3
            E.op("vector", "tensor_tensor", t3, src3, rb, ALU.mult)
            E.drain("vector")
            yield "apply1"
            E.op("vector", "tensor_tensor", dst3, t3, mrb, ALU.subtract,
                 inc="sV")
            E.gset("ST", "sV")
            if out is not None:
                out[0] = E.cnt["sV"]

        def emit_ln(*args, **kwargs):
            out = [None]
            for _ in emit_ln_gen(*args, out=out, **kwargs):
                pass
            return out[0]

        def adv(g):
            try:
                next(g)
            except StopIteration:
                pass

        for dst, src in [(cvec, t_cvec), (crow, t_crow), (fbTf, t_fbT),
                         (gatef, t_gate)]:
            E.dma("sync", dst[:], src[:, :], "dIn")
        E.dma("sync", v3(xact, NACT), t_x[:, NIN:TOK].rearrange(
            "(c p) t -> p c t", p=128), "dIn")
        c_dS_init = E.cnt["dIn"]

        E.dma("gpsimd", cvecb[:], t_cvec[:, :], "dGi")
        E.dma("gpsimd", identb[:], t_id[:, :], "dGi")
        E.dma("gpsimd", u2T[:], t_uT[:, :], "dGi")
        c_dG_init = E.cnt["dGi"]

        E.op("vector", "memset", zrow[:], 0.0)
        E.op("vector", "memset", lnscr[:], 0.0, inc="sV")
        c_memset = E.cnt["sV"]

        E.wait("tensor", "dIn", c_dS_init)
        E.wait("tensor", "dGi", c_dG_init)
        E.gwait("tensor", "ST")
        E.op("tensor", "matmul", ST[0:16, 0:1], crow[0:1, 0:16], gatef[:],
             start=True, stop=True, inc="sP")
        c_gb = E.cnt["sP"]
        E.wait("scalar", "sP", c_gb)
        E.op("scalar", "copy", g2c[:], ST[0:16, 0:1])
        E.drain("scalar")
        E.op("scalar", "activation", fbTg[:], fbTf[:], AF.Copy,
             scale=g2c[0:16, 0:1], inc="sA")
        c_fbTg = E.cnt["sA"]
        E.gset("ST", "sA")

        with ExitStack() as actx:
            asb = lambda n, s, dt: actx.enter_context(nc.sbuf_tensor(n, s, dt))
            wbuf = asb("wbuf", [128, DCN * 512], BF16)
            ctxXg = asb("ctxXg", [128, DCN * 64], F32)
            ctxn = asb("ctxn", [128, DCN * CTX], BF16)
            Kt = asb("Kt", [128, DCN * CTX], BF16)
            Qt = asb("Qt", [128, DCN * NACT], BF16)
            xnq = asb("xnq", [128, DCN * NACT], BF16)
            Sm = asb("Sm", [128, CTX], F32)
            AO = asb("AO", [128, DCN * NACT], F32)
            vnb = asb("vnb", [16, D], BF16)
            mwAs = asb("mwAs", [128, WIN], BF16)
            mwBs = asb("mwBs", [1, WIN], BF16)
            msAs = asb("msAs", [128, SINKS], BF16)
            msBs = asb("msBs", [1, SINKS], BF16)

            E.dma("gpsimd", vnb[:], t_vT[:, :], "dM")
            for dst, srct in [(mwAs, t_mwA), (mwBs, t_mwB), (msAs, t_msA),
                              (msBs, t_msB)]:
                E.dma("gpsimd", dst[:], srct[:, :], "dM")
            c_dG_masks = E.cnt["dM"]

            ctxn3 = v3(ctxn, CTX)
            Kt3 = v3(Kt, CTX)
            Qt3 = v3(Qt, NACT)
            xnq3 = v3(xnq, NACT)
            xact3 = v3(xact, NACT)
            AO3 = v3(AO, NACT)
            wbuf3 = v3(wbuf, 512)
            ctxXg3 = v3(ctxXg, 64)

            P_A = lnscr[:, 0:CTX]
            PTs = lnscr[:, 520:1160]
            PTb = lnscr[:, 1160:1165]
            Vh = lnscr[:, 1168:1168 + 5 * 64]
            O_s = lnscr[:, 1492:1492 + DCN * NACT]
            O_s3 = O_s.rearrange("p (c t) -> p c t", c=DCN)
            P_B = lnscr[0:1, 2524:2524 + CTX]
            plo = lnscr[:, 3072:4096]

            gsz = [64] * 8 + [4]
            gof = [64 * k for k in range(8)] + [512]
            for g in range(9):
                W = gsz[g]
                E.gwait("sync", "ctxXg")
                cL = E.dma("sync", ctxXg3[:, :, 0:W],
                           t_ctx[:, gof[g]:gof[g] + W].rearrange(
                               "(c p) t -> p c t", p=128), "dC")
                wl = [("dC", cL)] + ([("sV", c_memset)] if g == 0 else [])
                emit_ln(ctxXg3[:, :, 0:W], ctxn3[:, :, gof[g]:gof[g] + W], W,
                        src_wait=wl,
                        fscr3=Sm[:, 0:512].rearrange(
                            "p (c t) -> p c t", c=DCN)[:, :, 0:W])
                E.gset("ctxXg", "sV")
            c_ctxn = E.cnt["sV"]

            emit_ln(xact3, xnq3, NACT, src_wait=[("dIn", c_dS_init)],
                    fscr3=AO3)
            c_xnq = E.cnt["sV"]

            def load_wbuf(t_w, half):
                E.gwait("gpsimd", "wbuf")
                return E.dma("gpsimd", wbuf[:],
                             t_w[half * 128:(half + 1) * 128, :], "dW")

            big_i = [0]

            def next_big():
                i = big_i[0] % 2
                big_i[0] += 1
                return pbk[i], f"pbig{i}"

            att_units = []
            cw_h = {}

            def mk_K(dc2):
              def unit():
                if dc2 == 0:
                    cw_h["w"] = load_wbuf(t_wk, 0)
                if dc2 == 4:
                    cw_h["w"] = load_wbuf(t_wk, 1)
                c_w = cw_h["w"]
                bank, key = next_big()
                E.gwait("tensor", key)
                if dc2 in (0, 4):
                    E.wait("tensor", "dW", c_w)
                if dc2 == 0:
                    E.wait("tensor", "sV", c_ctxn)
                for dci in range(DCN):
                    E.op("tensor", "matmul", bank[:, 0:512],
                         wbuf3[:, dci, (dc2 % 4) * 128:(dc2 % 4) * 128 + 128],
                         ctxn3[:, dci, 4:516],
                         start=(dci == 0), stop=(dci == DCN - 1))
                E.gwait("tensor", "pbproj0")
                for dci in range(DCN):
                    E.op("tensor", "matmul", BC[:, 0:4],
                         wbuf3[:, dci, (dc2 % 4) * 128:(dc2 % 4) * 128 + 128],
                         ctxn3[:, dci, 0:4],
                         start=(dci == 0), stop=(dci == DCN - 1),
                         inc=("sP" if dci == DCN - 1 else None))
                cP = E.cnt["sP"]
                E.wait("scalar", "sP", cP)
                E.op("scalar", "activation", Kt3[:, dc2, 4:516], bank[:, 0:512],
                     AF.Copy)
                E.op("scalar", "activation", Kt3[:, dc2, 0:4], BC[:, 0:4],
                     AF.Copy, inc="sA")
                E.gset(key, "sA")
                E.gset("pbproj0", "sA")
                if dc2 == 3 or dc2 == DCN - 1:
                    E.gset("wbuf", "sP")
              return unit

            for dc2 in range(DCN):
                att_units.append(mk_K(dc2))

            def mk_Q(dc2):
              def unit():
                if dc2 == 0:
                    cw_h["w"] = load_wbuf(t_wq, 0)
                if dc2 == 4:
                    cw_h["w"] = load_wbuf(t_wq, 1)
                c_w = cw_h["w"]
                bank, key = next_big()
                E.gwait("tensor", key)
                if dc2 in (0, 4):
                    E.wait("tensor", "dW", c_w)
                if dc2 == 0:
                    E.wait("tensor", "sV", c_xnq)
                for dci in range(DCN):
                    E.op("tensor", "matmul", bank[:, 0:NACT],
                         wbuf3[:, dci, (dc2 % 4) * 128:(dc2 % 4) * 128 + 128],
                         xnq3[:, dci, 0:NACT],
                         start=(dci == 0), stop=(dci == DCN - 1),
                         inc=("sP" if dci == DCN - 1 else None))
                cP = E.cnt["sP"]
                E.wait("scalar", "sP", cP)
                E.op("scalar", "activation", Qt3[:, dc2, :], bank[:, 0:NACT],
                     AF.Copy, inc="sA")
                E.gset(key, "sA")
                if dc2 == DCN - 1:
                    cw_h["cQ"] = E.cnt["sA"]
                if dc2 == 3 or dc2 == DCN - 1:
                    E.gset("wbuf", "sP")
              return unit

            for dc2 in range(DCN):
                att_units.append(mk_Q(dc2))

            ksz = [SINKS, 128, 128, 128, 128]
            kof = [0, 4, 132, 260, 388]

            def mk_H(h):
              def unit():
                if h == 0:
                    cw_h["w"] = load_wbuf(t_wv, 0)
                if h == 8:
                    cw_h["w"] = load_wbuf(t_wv, 1)
                c_w = cw_h["w"]
                dc = h // 2
                pbq = 64 * (h % 2)
                E.gwait("tensor", "pb3")
                if h in (0, 8):
                    E.wait("tensor", "dW", c_w)
                if h == 0:
                    E.wait("tensor", "sA", E.cnt["sA"] if False else cw_h["cQ"])
                for kc in range(5):
                    for dci in range(DCN):
                        E.op("tensor", "matmul",
                             pbk[3][0:ksz[kc], 64 * kc:64 * kc + 64],
                             ctxn3[:, dci, kof[kc]:kof[kc] + ksz[kc]],
                             wbuf3[:, dci, (h % 8) * 64:(h % 8) * 64 + 64],
                             start=(dci == 0), stop=(dci == DCN - 1),
                             inc=("sP" if (kc == 4 and dci == DCN - 1) else None))
                cPv = E.cnt["sP"]
                E.gwait("scalar", "Vh")
                E.wait("scalar", "sP", cPv)
                for kc in range(5):
                    E.op("scalar", "activation",
                         Vh[0:ksz[kc], 64 * kc:64 * kc + 64],
                         pbk[3][0:ksz[kc], 64 * kc:64 * kc + 64], AF.Copy,
                         inc=("sA" if kc == 4 else None))
                E.gset("pb3", "sA")
                c_vh = E.cnt["sA"]
                if h == 7:
                    E.gset("wbuf", "sP")
                bank, key = next_big()
                E.gwait("tensor", key)
                E.op("tensor", "matmul", bank[:, 0:512],
                     Qt3[pbq:pbq + 64, dc, 0:128], Kt3[pbq:pbq + 64, dc, 4:516],
                     start=True, stop=True)
                E.gwait("tensor", "pbproj0")
                E.op("tensor", "matmul", BC[:, 0:4],
                     Qt3[pbq:pbq + 64, dc, 0:128], Kt3[pbq:pbq + 64, dc, 0:4],
                     start=True, stop=True)
                E.gwait("tensor", "pb4")
                E.op("tensor", "matmul", pbk[4][0:1, 0:512],
                     Qt3[pbq:pbq + 64, dc, 128:129], Kt3[pbq:pbq + 64, dc, 4:516],
                     start=True, stop=True)
                E.op("tensor", "matmul", BC[0:1, 8:12],
                     Qt3[pbq:pbq + 64, dc, 128:129], Kt3[pbq:pbq + 64, dc, 0:4],
                     start=True, stop=True, inc="sP")
                cPs = E.cnt["sP"]
                E.gwait("vector", "Sm")
                if h == 0:
                    E.wait("vector", "dM", c_dG_masks)
                E.wait("vector", "sP", cPs)
                E.op("vector", "tensor_tensor", Sm[:, 4:516], bank[:, 0:512],
                     mwAs[:], ALU.add)
                E.op("vector", "tensor_tensor", Sm[:, 0:4], BC[:, 0:4],
                     msAs[:], ALU.add, inc="sV")
                E.gset(key, "sV")
                E.drain("vector")
                E.op("vector", "reduce_max", nmb[:, 0:1], Sm[:, 0:516],
                     axis=AX.X, negate=True)
                E.drain("vector")
                E.op("vector", "tensor_scalar", nmb[:, 0:1], nmb[:, 0:1],
                     0.125, None, ALU.mult, inc="sV")
                cVm = E.cnt["sV"]
                E.gwait("vector", "srow")
                E.op("vector", "tensor_tensor", srow[0:1, 4:516],
                     pbk[4][0:1, 0:512], mwBs[:], ALU.add)
                E.op("vector", "tensor_tensor", srow[0:1, 0:4],
                     BC[0:1, 8:12], msBs[:], ALU.add, inc="sV")
                E.gset("pb4", "sV")
                E.gset("pbproj0", "sV")
                E.drain("vector")
                E.op("vector", "reduce_max", srow[0:1, 520:521],
                     srow[0:1, 0:516], axis=AX.X, negate=True)
                E.drain("vector")
                E.op("vector", "tensor_scalar", srow[0:1, 520:521],
                     srow[0:1, 520:521], 0.125, None, ALU.mult, inc="sV")
                cVb = E.cnt["sV"]
                E.gwait("scalar", "P")
                E.wait("scalar", "sV", cVm)
                E.op("scalar", "activation", P_A[:], Sm[:, 0:516], AF.Exp,
                     scale=0.125, bias=nmb[:, 0:1], inc="sA")
                E.gset("Sm", "sA")
                E.wait("scalar", "sV", cVb)
                E.op("scalar", "activation", P_B[:], srow[0:1, 0:516], AF.Exp,
                     scale=0.125, bias=srow[0:1, 520:521], inc="sA")
                cAe = E.cnt["sA"]
                E.gwait("vector", "Pn")
                E.wait("vector", "sA", cAe)
                E.op("vector", "reduce_sum", nmb[:, 1:2], P_A[:], axis=AX.X)
                E.op("vector", "reduce_sum", srow[0:1, 521:522], P_B[:],
                     axis=AX.X)
                E.drain("vector")
                E.op("vector", "reciprocal", nmb[:, 1:2], nmb[:, 1:2])
                E.op("vector", "reciprocal", srow[0:1, 521:522],
                     srow[0:1, 521:522])
                E.drain("vector")
                E.op("vector", "tensor_scalar", P_A[:], P_A[:], nmb[:, 1:2],
                     None, ALU.mult)
                E.op("vector", "tensor_scalar", P_B[:], P_B[:],
                     srow[0:1, 521:522], None, ALU.mult, inc="sV")
                E.gset("srow", "sV")
                cVp = E.cnt["sV"]
                E.gwait("tensor", "pt")
                E.wait("tensor", "sV", cVp)
                for kc in range(5):
                    E.op("tensor", "transpose",
                         pt[0:ksz[kc], 128 * kc:128 * kc + 128],
                         P_A[:, kof[kc]:kof[kc] + ksz[kc]], identb[:])
                for kc in range(5):
                    E.op("tensor", "transpose",
                         pt[0:ksz[kc], 640 + 2 * kc:641 + 2 * kc],
                         P_B[0:1, kof[kc]:kof[kc] + ksz[kc]],
                         identb[0:1, 0:1],
                         inc=("sP" if kc == 4 else None))
                cPt = E.cnt["sP"]
                E.gset("Pn", "sP")
                E.gwait("scalar", "PTs")
                E.gwait("scalar", "PTb")
                E.wait("scalar", "sP", cPt)
                E.op("scalar", "activation", PTs[:, 128:640], pt[:, 128:640],
                     AF.Copy)
                E.op("scalar", "activation", PTs[0:SINKS, 0:128],
                     pt[0:SINKS, 0:128], AF.Copy)
                E.op("scalar", "activation", PTb[:, 1:5],
                     pt[:, 642:650].rearrange("p (k two) -> p k two",
                                              two=2)[:, :, 0:1],
                     AF.Copy)
                E.op("scalar", "activation", PTb[0:SINKS, 0:1],
                     pt[0:SINKS, 640:641], AF.Copy, inc="sA")
                E.gwait("scalar", "PTb") if False else None
                E.gset("pt", "sA")
                E.gset("P", "sA")
                cAt = E.cnt["sA"]
                roff = 130 * (dc % 2)
                if h % 2 == 0:
                    E.gwait("tensor", f"pO{dc % 2}")
                E.wait("tensor", "sA", cAt)
                E.wait("tensor", "sA", c_vh)
                for kc in range(5):
                    E.op("tensor", "matmul",
                         pbk[5][pbq:pbq + 64, roff:roff + 128],
                         Vh[0:ksz[kc], 64 * kc:64 * kc + 64],
                         PTs[0:ksz[kc], 128 * kc:128 * kc + 128],
                         start=(kc == 0), stop=(kc == 4))
                for kc in range(5):
                    E.op("tensor", "matmul",
                         pbk[5][pbq:pbq + 64, roff + 128:roff + 129],
                         Vh[0:ksz[kc], 64 * kc:64 * kc + 64],
                         PTb[0:ksz[kc], kc:kc + 1],
                         start=(kc == 0), stop=(kc == 4),
                         inc=("sP" if kc == 4 else None))
                cPo = E.cnt["sP"]
                E.gset("Vh", "sP")
                E.gset("PTs", "sP")
                E.gset("PTb", "sP")
                if h == NH - 1:
                    E.gset("wbuf", "sP")
                if h % 2 == 1:
                    E.wait("scalar", "sP", cPo)
                    E.op("scalar", "activation", O_s3[:, dc, :],
                         pbk[5][:, roff:roff + 129], AF.Copy, inc="sA")
                    E.gset(f"pO{dc % 2}", "sA")
              return unit

            for h in range(NH):
                att_units.append(mk_H(h))
            import os
            if n_tiles < 0:
                NFCx = NFC if n_tiles == -3 else 0
                NPJx = 0
                n_tiles = 0
            else:
                NFCx, NPJx = NFC, NPJ
            NFCx = int(os.environ.get("KB_NFC", NFCx))
            NPJx = int(os.environ.get("KB_NPJ", NPJx))
            sfc_ready = []
            hsc_ready = []
            spj_ready = []
            osch_ready = []

            def mk_prep(kind, idx):
              def gen():
                slot = (idx if kind == "fc" else NFCx + idx) % 2
                stflat = stg[:, slot * 1024:slot * 1024 + 1024]
                if kind == "fc":
                    red_c, red_w, inv_n = 4, 128, 1.0 / D
                    src_t = t_wfc[idx * 128:(idx + 1) * 128, :]
                    st3 = stg3a if slot == 0 else stg3b
                    cn = DCN
                else:
                    red_c, red_w, inv_n = 16, 32, 1.0 / HF
                    src_t = t_wpj[idx * 128:(idx + 1) * 128, :]
                    st3 = stflat.rearrange("p (c t) -> p c t", c=HCN)
                    cn = HCN
                sga = sgn1[:, 0:1024]
                sgb = sgn1[:, 1024:2048]
                E.gwait2(("sync", "scalar"), f"stg{slot}")
                cL = E.dma("sync", stflat[:, 0:512], src_t[:, 0:512],
                           f"dS{slot}")
                cL2 = E.dma("scalar", stflat[:, 512:1024],
                            src_t[:, 512:1024], "dC")
                E.gwait("scalar", "sgn1a")
                E.wait("scalar", f"dS{slot}", cL)
                E.wait("scalar", "dC", cL2)
                E.op("scalar", "activation", sga, stflat, AF.Relu,
                     scale=1.0, inc="sA")
                cHiA = E.cnt["sA"]
                E.gwait("vector", "plo")
                E.wait("vector", "sA", cHiA)
                E.op("vector", "scalar_tensor_tensor", plo[:, 0:1024],
                     stflat, 0.0, sga, ALU.max, ALU.subtract, inc="sV")
                cLoA = E.cnt["sV"]
                yield "A1a"
                E.wait("tensor", "sA", cHiA)
                E.wait("tensor", "sV", cLoA)
                E.gwait("tensor", "ST")
                E.gwait("tensor", "pb4")
                E.op("tensor", "matmul", ST[0:1, 0:512], ones_b,
                     sga[:, 0:512], start=True, stop=False)
                E.op("tensor", "matmul", pbk[4][0:1, 0:512], ones_b,
                     sga[:, 512:1024], start=True, stop=False)
                E.op("tensor", "matmul", ST[0:1, 0:512], ones_b,
                     plo[:, 0:512], start=False, stop=False)
                E.op("tensor", "matmul", pbk[4][0:1, 0:512], ones_b,
                     plo[:, 512:1024], start=False, stop=False, inc="sP")
                E.gset("plo", "sP")
                E.gset("sgn1a", "sP")
                E.gwait("scalar", "sgn1b")
                E.op("scalar", "activation", sgb, stflat, AF.Relu,
                     scale=-1.0, inc="sA")
                cHiB = E.cnt["sA"]
                E.gwait("vector", "plo")
                E.wait("vector", "sA", cHiB)
                E.op("vector", "scalar_tensor_tensor", plo[:, 0:1024],
                     stflat, 0.0, sgb, ALU.min, ALU.add, inc="sV")
                cLoB = E.cnt["sV"]
                E.wait("tensor", "sA", cHiB)
                E.wait("tensor", "sV", cLoB)
                E.op("tensor", "matmul", ST[0:1, 0:512], ones_b,
                     sgb[:, 0:512], start=False, stop=False)
                E.op("tensor", "matmul", pbk[4][0:1, 0:512], ones_b,
                     sgb[:, 512:1024], start=False, stop=False)
                E.op("tensor", "matmul", ST[0:1, 0:512], negones_b,
                     plo[:, 0:512], start=False, stop=True)
                E.op("tensor", "matmul", pbk[4][0:1, 0:512], negones_b,
                     plo[:, 512:1024], start=False, stop=True, inc="sP")
                E.gset("plo", "sP")
                E.gset("sgn1b", "sP")
                cSt = E.cnt["sP"]
                yield "A1b"
                E.gwait("vector", "srowp")
                E.wait("vector", "sP", cSt)
                v1 = ST[0:1, 0:512].rearrange("p (c h) -> p h c", c=red_c)
                v2 = pbk[4][0:1, 0:512].rearrange("p (c h) -> p h c",
                                                  c=red_c)
                E.op("vector", "tensor_reduce", srowp[0:1, 0:red_w], v1,
                     AX.X, ALU.add)
                E.op("vector", "tensor_reduce",
                     srowp[0:1, 256:256 + red_w], v2, AX.X, ALU.add,
                     inc="sV")
                E.gset("pb4", "sV")
                E.drain("vector")
                E.op("vector", "tensor_add", srowp[0:1, 0:red_w],
                     srowp[0:1, 0:red_w], srowp[0:1, 256:256 + red_w])
                E.drain("vector")
                E.op("vector", "tensor_scalar", srowp[0:1, 0:red_w],
                     srowp[0:1, 0:red_w], inv_n, 1e-5, ALU.mult, ALU.max)
                E.drain("vector")
                E.op("vector", "reciprocal", srowp[0:1, 128:128 + red_w],
                     srowp[0:1, 0:red_w], inc="sV")
                cRw = E.cnt["sV"]
                yield "A2a"
                if kind == "fc":
                    E.wait("scalar", "sV", cRw)
                    E.op("scalar", "activation", srowp[0:1, 256:384],
                         srowp[0:1, 0:128], AF.Square, inc="sA")
                    cH2 = E.cnt["sA"]
                    E.wait("tensor", "sA", cH2)
                    E.op("tensor", "matmul", ST[:, 128:256], crow[:],
                         srowp[0:1, 128:256], start=True, stop=True)
                    E.op("tensor", "transpose", ST[:, 384:385],
                         srowp[0:1, 256:384], idf1, inc="sP")
                    cBc = E.cnt["sP"]
                    E.gset("srowp", "sP")
                    E.wait("scalar", "sP", cBc)
                    E.gwait("scalar", f"rbb{slot}")
                    E.op("scalar", "activation", hsc2[:, idx:idx + 1],
                         ST[:, 384:385], AF.Copy, inc="sA")
                    hsc_ready.append(E.cnt["sA"])
                    E.op("scalar", "activation",
                         rbb[:, slot * 128:slot * 128 + 128],
                         ST[:, 128:256], AF.Copy, inc="sA")
                else:
                    E.wait("tensor", "sV", cRw)
                    E.op("tensor", "matmul", ST[:, 128:160], crow[:],
                         srowp[0:1, 128:160], start=True, stop=True)
                    E.op("tensor", "transpose", ST[0:32, 384:385],
                         srowp[0:1, 0:32], idf1, inc="sP")
                    cBc = E.cnt["sP"]
                    E.gset("srowp", "sP")
                    E.wait("scalar", "sP", cBc)
                    E.gwait("scalar", f"rbb{slot}")
                    oc = idx // 4
                    rlo = 32 * (idx % 4)
                    E.op("scalar", "activation",
                         osch[rlo:rlo + 32, oc:oc + 1],
                         ST[0:32, 384:385], AF.Copy, inc="sA")
                    osch_ready.append(E.cnt["sA"])
                    E.op("scalar", "activation",
                         rbb[:, slot * 128:slot * 128 + 32],
                         ST[:, 128:160], AF.Copy, inc="sA")
                cRb = E.cnt["sA"]
                E.gset("ST", "sA")
                yield "A2b"
                E.wait("vector", "sA", cRb)
                w = 128 if kind == "fc" else 32
                rb3 = rbb[:, slot * 128:slot * 128 + w].unsqueeze(
                    1).broadcast_to((128, cn, w))
                if kind == "fc":
                    E.op("vector", "tensor_tensor", st3, st3, rb3, ALU.mult)
                    if idx == 0:
                        E.wait("vector", "dGi", c_dG_init)
                    E.op("vector", "tensor_tensor",
                         u2T[0:16, idx * 128:(idx + 1) * 128],
                         u2T[0:16, idx * 128:(idx + 1) * 128],
                         rbb[0:16, slot * 128:slot * 128 + 128], ALU.mult,
                         inc="sV")
                else:
                    E.op("vector", "tensor_tensor", st3, st3, rb3, ALU.mult,
                         inc="sV")
                E.gset(f"rbb{slot}", "sV")
                E.drain("vector")
                E.gwait("vector", "sgn1b")
                E.op("vector", "tensor_single_scalar", sgb, stflat, -0.5,
                     ALU.is_lt, inc="sV")
                E.drain("vector")
                if kind == "fc":
                    E.op("vector", "scalar_tensor_tensor",
                         sfc3[:, :, idx * 128:(idx + 1) * 128], st3, 0.5,
                         sgb.rearrange("p (c t) -> p c t", c=DCN),
                         ALU.is_gt, ALU.subtract, inc="sV")
                    sfc_ready.append(E.cnt["sV"])
                else:
                    E.op("vector", "scalar_tensor_tensor",
                         spj3[:, :, idx * 32:(idx + 1) * 32], st3, 0.5,
                         sgb.rearrange("p (c t) -> p c t", c=HCN),
                         ALU.is_gt, ALU.subtract, inc="sV")
                    spj_ready.append(E.cnt["sV"])
                E.gset(f"stg{slot}", "sV")
                E.gset("sgn1b", "sV")
              return gen()

            prep_units = []
            for ph in range(NFCx):
                prep_units.append(("fc", ph))
            for po in range(NPJx):
                prep_units.append(("pj", po))

            pi = 0
            prev_g = [None]

            cur_g = [None]

            def prep_step():
                nonlocal pi
                if cur_g[0] is None:
                    if pi >= len(prep_units):
                        return
                    cur_g[0] = mk_prep(*prep_units[pi])
                    pi += 1
                    next(cur_g[0])
                g = cur_g[0]
                if prev_g[0] is not None:
                    adv(prev_g[0])
                next(g)
                next(g)
                next(g)
                if pi < len(prep_units):
                    cur_g[0] = mk_prep(*prep_units[pi])
                    pi += 1
                    next(cur_g[0])
                else:
                    cur_g[0] = None
                prev_g[0] = g

            for ui, unit in enumerate(att_units):
                unit()
                prep_step()
                prep_step()
            if prev_g[0] is not None:
                adv(prev_g[0])
            while pi < len(prep_units):
                prep_units[pi]()
                pi += 1
            c_u2T = sfc_ready[-1] if sfc_ready else 0
            c_spj = spj_ready[-1] if spj_ready else 0

            c_wlr = 0
            vfbT = Kt[0:16, 0:1024]
            if NFCx == NFC:
                E.gwait("tensor", "ST")
                E.wait("tensor", "sA", c_fbTg)
                E.wait("tensor", "dM", c_dG_masks)
                for hf2 in range(2):
                    if hf2 == 1:
                        E.wait("tensor", "sA", E.cnt["sA"])
                    E.op("tensor", "matmul", ST[0:16, 0:512], fbTg[:],
                         vnb[0:16, hf2 * 512:hf2 * 512 + 512],
                         start=True, stop=True, inc="sP")
                    cM1 = E.cnt["sP"]
                    E.wait("scalar", "sP", cM1)
                    E.op("scalar", "activation",
                         vfbT[0:16, hf2 * 512:hf2 * 512 + 512],
                         ST[0:16, 0:512], AF.Copy, inc="sA")
                cV1 = E.cnt["sA"]
                E.gset("ST", "sA")
                E.wait("tensor", "sA", cV1)
                E.wait("tensor", "sV", c_u2T)
                for dc in range(DCN):
                    for hh in range(DCN):
                        bank, key = next_big()
                        E.gwait("tensor", key)
                        E.op("tensor", "matmul", bank[:, 0:512],
                             vfbT[0:16, dc * 128:dc * 128 + 128],
                             u2T[0:16, hh * 512:hh * 512 + 512],
                             start=True, stop=True, inc="sP")
                        cX = E.cnt["sP"]
                        sl = s_fc[:, dc * HF + hh * 512:dc * HF + hh * 512 + 512]
                        E.wait("vector", "sP", cX)
                        E.op("vector", "tensor_tensor", sl, sl, bank[:, 0:512],
                             ALU.add, inc="sV")
                        E.gset(key, "sV")
                c_wlr = E.cnt["sV"]
            c_O = E.cnt["sA"]

            c_w = load_wbuf(t_wo, 0)
            for oc2 in range(DCN):
                if oc2 == 4:
                    c_w = load_wbuf(t_wo, 1)
                bank, key = next_big()
                E.gwait("tensor", key)
                if oc2 in (0, 4):
                    E.wait("tensor", "dW", c_w)
                if oc2 == 0:
                    E.wait("tensor", "sA", c_O)
                for dci in range(DCN):
                    E.op("tensor", "matmul", bank[:, 0:NACT],
                         wbuf3[:, dci, (oc2 % 4) * 128:(oc2 % 4) * 128 + 128],
                         O_s3[:, dci, 0:NACT],
                         start=(dci == 0), stop=(dci == DCN - 1),
                         inc=("sP" if dci == DCN - 1 else None))
                cP = E.cnt["sP"]
                E.wait("vector", "sP", cP)
                E.op("vector", "tensor_copy", AO3[:, oc2, :], bank[:, 0:NACT],
                     inc="sV")
                E.gset(key, "sV")
                if oc2 == 3 or oc2 == DCN - 1:
                    E.gset("wbuf", "sP")
            c_AO = E.cnt["sV"]
            E.gset("lnscr", "sP")

            if n_tiles >= 0 or n_tiles in (-2, -3):
                emit_ln(AO3, AO3, NACT, src_wait=[("sV", c_AO)])
                E.drain("vector")
                E.op("vector", "tensor_tensor", xact3, xact3, AO3, ALU.add,
                     inc="sV")
            c_attn = E.cnt["sV"]



        with ExitStack() as bctx:
            bsb = lambda n, s, dt: bctx.enter_context(nc.sbuf_tensor(n, s, dt))
            xtl = [bsb(f"xtl{i}", [128, DCN * TT], F32) for i in range(2)]
            xn_t = [bsb(f"xn{i}", [128, DCN * TT], BF16) for i in range(2)]
            fsq = bsb("fsq", [128, DCN * 512], BF16)
            fscr = stg
            E.gset("fscr", "sV")
            r2s = [bsb(f"r2s{i}", [128, TT], BF16) for i in range(6)]
            rpp = [bsb(f"rpp{i}", [128, TT], F32) for i in range(2)]

            xn3_t = [v3(x, TT) for x in xn_t]
            fsq3 = v3(fsq, 512)
            fscr3 = v3(fscr, TT)
            xact3 = v3(xact, NACT)


            xload = {}

            def emit_xload(j):
                if j >= n_tiles:
                    return
                if j == 0:
                    E.wait("sync", "sV", c_attn)
                    E.wait("gpsimd", "sV", c_attn)
                if j < 2:
                    E.wait("sync", f"dS{j % 2}", E.cnt[f"dS{j % 2}"])
                E.gwait2(("sync", "gpsimd"), f"xtl{j % 2}")
                xv = v3(xtl[j % 2], TT)
                ca = E.dma("sync", xv[:, 0:4, :],
                           t_x[0:512, j * TT:(j + 1) * TT].rearrange(
                               "(c p) t -> p c t", p=128), f"dS{j % 2}")
                cb = E.dma("gpsimd", xv[:, 4:8, :],
                           t_x[512:1024, j * TT:(j + 1) * TT].rearrange(
                               "(c p) t -> p c t", p=128),
                           "dGi" if j % 2 == 0 else "dM")
                xload[j] = (ca, cb)

            emit_xload(0)
            emit_xload(1)

            c_store = []

            def ln1_start(jj):
                xfj3 = v3(xtl[jj % 2], TT)
                merge_wait = []
                if jj == n_tiles - 1 and n_tiles == NTILE:
                    E.wait("vector", f"dS{jj % 2}", xload[jj][0])
                    E.wait("vector", "dGi" if jj % 2 == 0 else "dM",
                           xload[jj][1])
                    E.wait("vector", "sV", c_attn)
                    E.op("vector", "tensor_copy", xfj3[:, :, TT - NACT:TT],
                         xact3, inc="sV")
                    merge_wait = [("sV", E.cnt["sV"])]
                    E.drain("vector")
                out = [None]
                g = emit_ln_gen(xfj3, xn3_t[jj % 2], TT,
                                src_wait=[(f"dS{jj % 2}", xload[jj][0]),
                                          ("dGi" if jj % 2 == 0 else "dM",
                                           xload[jj][1])]
                                + merge_wait,
                                dst_key=f"xn{jj % 2}", fscr3=fscr3,
                                cast_split=True, out=out, t3_key="fscr")
                next(g)
                return g, out

            r2_of = {}
            tile_state = {}

            def emit_fc(j, hc):
                bank = pbk[hc % 2]
                key = f"pbig{hc % 2}"
                E.gwait("tensor", key)
                for dci in range(DCN):
                    E.op("tensor", "matmul", bank[:, 0:TT],
                         s_fc[:, dci * HF + hc * 128:dci * HF + hc * 128 + 128],
                         xn_t[j % 2][:, dci * TT:(dci + 1) * TT],
                         start=(dci == 0), stop=(dci == DCN - 1),
                         inc=("sP" if dci == DCN - 1 else None))
                cF = E.cnt["sP"]
                if hc % 2 == 0:
                    E.gwait("scalar", f"rpp{hc % 2}")
                    E.wait("scalar", "sP", cF)
                    if j == 0 and hc == 0:
                        E.wait("scalar", "sA", hsc_ready[-1])
                    E.op("scalar", "activation", rpp[hc % 2][:],
                         bank[:, 0:TT], AF.Relu,
                         scale=hsc2[:, hc:hc + 1], inc="sA")
                    cR = E.cnt["sA"]
                    E.gwait("vector", f"r2s{hc % 6}")
                    E.wait("vector", "sP", cF)
                    E.wait("vector", "sA", cR)
                else:
                    E.gwait("vector", f"rpp{hc % 2}")
                    E.wait("vector", "sP", cF)
                    E.op("vector", "tensor_scalar", rpp[hc % 2][:],
                         bank[:, 0:TT], hsc2[:, hc:hc + 1], 0.0,
                         ALU.mult, ALU.max)
                    E.drain("vector")
                    E.gwait("vector", f"r2s{hc % 6}")
                E.op("vector", "tensor_tensor", r2s[hc % 6][:],
                     bank[:, 0:TT], rpp[hc % 2][:], ALU.mult, inc="sV")
                r2_of[j][hc] = E.cnt["sV"]
                E.gset(key, "sV")
                E.gset(f"rpp{hc % 2}", "sV")

            def emit_proj(j, hc):
                E.wait("tensor", "sV", r2_of[j][hc])
                for oc in range(OCN):
                    bank = pbk[2 + oc // 2]
                    E.op("tensor", "matmul",
                         bank[:, (oc % 2) * TT:(oc % 2) * TT + TT],
                         s_pj[:, hc * D + oc * 128:hc * D + oc * 128 + 128],
                         r2s[hc % 6][:], start=False,
                         stop=(hc == HCN - 1 and oc % 2 == 1),
                         inc=("sP" if oc == OCN - 1 else None))
                E.gset(f"r2s{hc % 6}", "sP")

            def fc_head(j):
                r2_of[j] = [0] * HCN
                E.wait("tensor", "sV", tile_state.pop(j))
                if j == 0:
                    E.wait("tensor", "sV", c_wlr)
                    E.wait("tensor", "sV", c_u2T)
                for hc in range(4):
                    emit_fc(j, hc)

            def ln2_tail(p, g2):
                xfp3 = v3(xtl[p % 2], TT)
                E.drain("vector")
                E.op("vector", "tensor_tensor", fscr3, xfp3, fscr3, ALU.add,
                     inc="sV")
                cOut = E.cnt["sV"]
                E.gset(f"xtl{p % 2}", "sV")
                E.drain("vector")
                E.wait("scalar", "sV", cOut)
                cS = E.dma("scalar",
                           t_out[:, p * TT:(p + 1) * TT].rearrange(
                               "(c p) t -> p c t", p=128), fscr3, f"dO{p % 2}")
                c_store.append((f"dO{p % 2}", cS))
                E.gset("fscr", f"dO{p % 2}")

            if n_tiles > 0:
                g0, o0 = ln1_start(0)
                for _ in g0:
                    pass
                E.gset("fscr", "sV")
                tile_state[0] = o0[0]
                fc_head(0)

            pend2 = None
            ln1 = None
            for j in range(n_tiles):
                emit_fc(j, 4)
                emit_fc(j, 5)
                if pend2 is not None:
                    adv(pend2)
                for bnk in range(4):
                    E.gwait("tensor", f"pbproj{bnk}")
                    if j == 0 and bnk == 0:
                        E.wait("tensor", "sV", c_spj)
                    E.op("tensor", "matmul", pbk[2 + bnk][:, 0:256],
                         zrow[0:1, 0:128], zrow[0:1, 0:256],
                         start=True, stop=False)
                    E.op("tensor", "matmul", pbk[2 + bnk][:, 256:512],
                         zrow[0:1, 0:128], zrow[0:1, 0:256],
                         start=False, stop=False)
                for hc in range(6, HCN):
                    emit_proj(j, hc - 6)
                    emit_fc(j, hc)
                    if pend2 is not None:
                        if hc == 7:
                            adv(pend2)
                        elif hc == 9:
                            adv(pend2)
                        elif hc == 11:
                            adv(pend2)
                        elif hc == 13:
                            adv(pend2)
                        elif hc == 15:
                            ln2_tail(j - 1, pend2)
                            pend2 = None
                    if hc == 15 and j >= 1:
                        emit_xload(j + 1)
                    if j + 1 < n_tiles:
                        if hc == 18:
                            ln1 = ln1_start(j + 1)
                        elif hc == 23:
                            adv(ln1[0])
                        elif hc == 26:
                            adv(ln1[0])
                        elif hc == 27:
                            adv(ln1[0])
                        elif hc == 28:
                            adv(ln1[0])
                        elif hc == 29:
                            adv(ln1[0])
                            E.gset("fscr", "sV")
                            tile_state[j + 1] = ln1[1][0]
                E.gset(f"xn{j % 2}", "sP")
                for hc in range(HCN - 6, HCN):
                    emit_proj(j, hc)
                c_proj = E.cnt["sP"]
                if j + 1 < n_tiles:
                    fc_head(j + 1)
                E.gwait2(("scalar", "vector"), "fsq")
                E.gwait2(("scalar", "vector"), "fscr")
                E.wait("scalar", "sP", c_proj)
                if j == 0:
                    E.wait("scalar", "sA", osch_ready[-1])
                for oc in range(4):
                    bank = pbk[2 + oc // 2]
                    reg = bank[:, (oc % 2) * TT:(oc % 2) * TT + TT]
                    E.op("scalar", "activation", fsq3[:, oc, 0:TT], reg,
                         AF.Copy, scale=osch[:, oc:oc + 1])
                    E.op("scalar", "activation", fsq3[:, oc, 256:256 + TT],
                         reg, AF.Square, scale=osch[:, oc:oc + 1])
                    E.op("scalar", "activation", fscr3[:, oc, :], reg,
                         AF.Copy, scale=osch[:, oc:oc + 1],
                         inc=("sA" if oc == 3 else None))
                cC2a = E.cnt["sA"]
                E.gset("pbproj0", "sA")
                E.gset("pbproj1", "sA")
                E.wait("vector", "sP", c_proj)
                for oc in range(4, OCN):
                    bank = pbk[2 + oc // 2]
                    reg = bank[:, (oc % 2) * TT:(oc % 2) * TT + TT]
                    E.op("vector", "tensor_scalar", fscr3[:, oc, :], reg,
                         osch[:, oc:oc + 1], None, ALU.mult,
                         inc=("sV" if oc == OCN - 1 else None))
                E.gset("pbproj2", "sV")
                E.gset("pbproj3", "sV")
                E.drain("vector")
                for oc in range(4, OCN):
                    E.op("vector", "tensor_copy", fsq3[:, oc, 0:TT],
                         fscr3[:, oc, :])
                    E.op("vector", "tensor_tensor",
                         fsq3[:, oc, 256:256 + TT], fscr3[:, oc, :],
                         fscr3[:, oc, :], ALU.mult,
                         inc=("sV" if oc == OCN - 1 else None))
                cC2v = E.cnt["sV"]
                o2 = [None]
                g2 = emit_ln_gen(fscr3, fscr3, TT, precast=True, scr=fsq,
                                 scr_key="fsq",
                                 stat_waits=[("sV", cC2v)], out=o2)
                next(g2)
                pend2 = g2

            if pend2 is not None:
                adv(pend2)
                adv(pend2)
                adv(pend2)
                adv(pend2)
                adv(pend2)
                ln2_tail(n_tiles - 1, pend2)

            E.wait("scalar", "dO0", E.cnt["dO0"])
            E.wait("scalar", "dO1", E.cnt["dO1"])

        sems = {}
        with ExitStack() as sctx:
            for s in ("dIn", "dS0", "dS1", "dO0", "dO1", "dGi", "dM", "dC",
                      "dW", "sP", "sA", "sV", "sG"):
                sems[s] = sctx.enter_context(nc.semaphore(s))
            block = sctx.enter_context(nc.Block())

            @block.sync
            def _(sync):
                E.replay("sync", sync, sems)

            @block.gpsimd
            def _(gp):
                E.replay("gpsimd", gp, sems)

            @block.tensor
            def _(te):
                E.replay("tensor", te, sems)

            @block.scalar
            def _(sc):
                E.replay("scalar", sc, sems)

            @block.vector
            def _(ve):
                E.replay("vector", ve, sems)

    return nc


def _stage_inputs(L0_attn_x, Wq, Wk, Wv, Wo, fast_b, w_fc, w_proj, u, v, gate):
    x = np.ascontiguousarray(np.asarray(L0_attn_x, dtype=np.float32))
    def _stw(W):
        W = np.asarray(W, np.float32)
        return np.ascontiguousarray(
            W.reshape(DCN, 128, 2, 512).transpose(2, 1, 0, 3).reshape(
                256, DCN * 512))

    shared = {
        "wq": _stw(Wq),
        "wk": _stw(Wk),
        "wv": _stw(Wv),
        "wo": _stw(Wo),
        "uT": np.ascontiguousarray(np.asarray(u, np.float32).T),
        "vT": np.ascontiguousarray(np.asarray(v, np.float32)),
        "fbT": np.ascontiguousarray(np.asarray(fast_b, np.float32).T),
        "gate": np.asarray(gate, np.float32).reshape(1, 1),
        "cvec": np.tile(np.array([[1.0, 0.5, -0.5, EPS, 2.0, -1.0, 0, 0]],
                                 np.float32), (128, 1)),
        "crow": np.ones((1, 128), np.float32),
        "ident": np.eye(128, dtype=np.float32),
    }
    wfcT = np.asarray(w_fc, np.float32).T
    shared["wfc_st"] = np.ascontiguousarray(
        wfcT.reshape(DCN, 128, NFC, FCP).transpose(2, 1, 0, 3).reshape(
            NFC * 128, DCN * FCP))
    wpjT = np.asarray(w_proj, np.float32).T
    shared["wpj_st"] = np.ascontiguousarray(
        wpjT.reshape(HCN, 128, NPJ, PJP).transpose(2, 1, 0, 3).reshape(
            NPJ * 128, HCN * PJP))

    in_maps = []
    for core in range(8):
        beta, i = divmod(core, CPB)
        m = dict(shared)
        inact = x[beta, NIN * i:NIN * (i + 1)]
        act = x[beta, ACT0 + NACT * i:ACT0 + NACT * (i + 1)]
        shard = np.concatenate([inact, act], axis=0)
        m["xT"] = np.ascontiguousarray(shard.T)
        ctx = np.concatenate([x[beta, 0:SINKS], x[beta, T - WIN:T]], axis=0)
        m["ctxT"] = np.ascontiguousarray(ctx.T)
        rr = ACT0 + NACT * i + np.arange(NACT)[:, None]
        jw = np.arange(WIN)[None, :]
        mw = np.where(jw <= rr - (T - WIN), 0.0, NEG).astype(np.float32)
        cs = np.arange(SINKS)[None, :]
        ms = np.where(cs <= rr - ACT0, 0.0, NEG).astype(np.float32)
        m["mwA"] = np.ascontiguousarray(mw[0:128])
        m["mwB"] = np.ascontiguousarray(mw[128:129])
        m["msA"] = np.ascontiguousarray(ms[0:128])
        m["msB"] = np.ascontiguousarray(ms[128:129])
        in_maps.append(m)
    return in_maps


def kernel(L0_attn_x, L0_attn_Wq, L0_attn_Wk, L0_attn_Wv, L0_attn_Wo,
           L0_ffn_fast_b, L0_ffn_w_fc, L0_ffn_w_proj, L0_ffn_u, L0_ffn_v,
           L0_ffn_gate):
    in_maps = _stage_inputs(L0_attn_x, L0_attn_Wq, L0_attn_Wk, L0_attn_Wv,
                            L0_attn_Wo, L0_ffn_fast_b, L0_ffn_w_fc,
                            L0_ffn_w_proj, L0_ffn_u, L0_ffn_v, L0_ffn_gate)
    nc = build_nc()
    res = run_bass_kernel_spmd(nc, in_maps, core_ids=list(range(8)))
    out = np.empty((B, T, D), dtype=np.float32)
    for core in range(8):
        beta, i = divmod(core, CPB)
        oc = np.asarray(res.results[core]["outT"]).T
        out[beta, NIN * i:NIN * (i + 1)] = oc[0:NIN]
        out[beta, ACT0 + NACT * i:ACT0 + NACT * (i + 1)] = oc[NIN:TOK]
    return out


# version 24 (final)
# speedup vs baseline: 1.0897x; 1.0897x over previous
import numpy as np
from contextlib import ExitStack

import concourse.bass as bass
import concourse.mybir as mybir
from concourse.bass_utils import run_bass_kernel_spmd

F32 = mybir.dt.float32
BF16 = mybir.dt.bfloat16
AF = mybir.ActivationFunctionType
ALU = mybir.AluOpType
AX = mybir.AxisListType

B, T, D = 2, 16384, 1024
HF, R = 4096, 16
NH, DH, SINKS, WIN, CTX = 16, 64, 4, 512, 516
CPB = 4
ACT_TOT = 516
ACT0 = T - ACT_TOT
NACT = ACT_TOT // CPB
NIN = ACT0 // CPB
TOK = NIN + NACT
EPS = 1e-5
NEG = -1.0e9

TT = 256
NTILE = TOK // TT
DCN = D // 128
HCN = HF // 128
OCN = D // 128
FCP = 128
PJP = 32
NFC = HF // FCP
NPJ = D // PJP

ENGS = ("tensor", "scalar", "vector", "gpsimd", "sync")


class Em:

    def __init__(self):
        self.q = {e: [] for e in ENGS}
        self.cnt = {}
        self.guards = {}

    def sem(self, name):
        self.cnt[name] = 0

    def inc(self, name, n=1):
        self.cnt[name] += n
        return self.cnt[name]

    def wait(self, eng, sem, val):
        if val > self.cnt[sem]:
            raise RuntimeError(f"wait {sem}>={val}, emitted {self.cnt[sem]}")
        if val > 0:
            self.q[eng].append(("w", sem, val))

    ENG_SEM = {"vector": "sV", "scalar": "sA", "tensor": "sP",
               "gpsimd": "sG"}

    def drain(self, eng):
        sem = self.ENG_SEM.get(eng)
        if sem is not None and sem in self.cnt:
            for idx in range(len(self.q[eng]) - 1, -1, -1):
                a = self.q[eng][idx]
                if a[0] == "o":
                    if a[4] is None:
                        self.inc(sem)
                        self.q[eng][idx] = (a[0], a[1], a[2], a[3], (sem, 1))
                        self.q[eng].append(("w", sem, self.cnt[sem]))
                        return
                    if a[4][0] == sem:
                        self.q[eng].append(("w", sem, self.cnt[sem]))
                        return
                    break
                if a[0] in ("w", "d"):
                    continue
            self.q[eng].append(("d",))
        else:
            self.q[eng].append(("d",))

    def op(self, eng, fname, *args, inc=None, **kwargs):
        if inc is not None and not isinstance(inc, tuple):
            inc = (inc, 1)
        if inc is not None:
            self.inc(inc[0], inc[1])
        self.q[eng].append(("o", fname, args, kwargs, inc))
        return self.cnt[inc[0]] if inc else None

    def dma(self, eng, out, in_, sem, **kwargs):
        return self.op(eng, "dma_start", inc=(sem, 16), out=out, in_=in_, **kwargs)

    def gwait(self, eng, key):
        for sem, val in self.guards.pop(key, []):
            self.wait(eng, sem, val)

    def gwait2(self, engs, key):
        for sem, val in self.guards.pop(key, []):
            for e in engs:
                self.wait(e, sem, val)

    def gset(self, key, sem):
        self.guards.setdefault(key, []).append((sem, self.cnt[sem]))

    def replay(self, eng_name, engobj, sems):
        for act in self.q[eng_name]:
            if act[0] == "w":
                engobj.wait_ge(sems[act[1]], act[2])
            elif act[0] == "d":
                engobj.drain()
            else:
                _, fname, args, kwargs, incv = act
                inst = getattr(engobj, fname)(*args, **kwargs)
                if incv is not None:
                    inst.then_inc(sems[incv[0]], incv[1])


def build_nc(n_tiles=NTILE):
    nc = bass.Bass("TRN2", target_bir_lowering=False)

    DI = lambda n, s: nc.dram_tensor(n, s, F32, kind="ExternalInput")
    t_x = DI("xT", [D, TOK])
    t_ctx = DI("ctxT", [D, CTX])
    t_mwA = DI("mwA", [128, WIN])
    t_mwB = DI("mwB", [1, WIN])
    t_msA = DI("msA", [128, SINKS])
    t_msB = DI("msB", [1, SINKS])
    t_cvec = DI("cvec", [128, 8])
    t_crow = DI("crow", [1, 128])
    t_id = DI("ident", [128, 128])
    t_wq = DI("wq", [2 * 128, DCN * 512])
    t_wk = DI("wk", [2 * 128, DCN * 512])
    t_wv = DI("wv", [2 * 128, DCN * 512])
    t_wo = DI("wo", [2 * 128, DCN * 512])
    t_wfc = DI("wfc_st", [NFC * 128, DCN * FCP])
    t_wpj = DI("wpj_st", [NPJ * 128, HCN * PJP])
    t_uT = DI("uT", [R, HF])
    t_vT = DI("vT", [R, D])
    t_fbT = DI("fbT", [R, R])
    t_gate = DI("gate", [1, 1])
    t_out = nc.dram_tensor("outT", [D, TOK], F32, kind="ExternalOutput")

    E = Em()
    for s in ("dIn", "dS0", "dS1", "dO0", "dO1", "dGi", "dM", "dC", "dW",
              "sP", "sA", "sV", "sG"):
        E.sem(s)

    with ExitStack() as octx:
        sbuf = lambda n, s, dt: octx.enter_context(nc.sbuf_tensor(n, s, dt))
        s_fc = sbuf("Sfc", [128, DCN * HF], BF16)
        s_pj = sbuf("Spj", [128, HCN * D], BF16)
        stg = sbuf("stg", [128, 2048], F32)
        sgn1 = sbuf("sgn1", [128, 2048], BF16)
        u2T = sbuf("u2T", [16, HF], BF16)
        srow = sbuf("srow", [1, 768], F32)
        srowp = sbuf("srowp", [1, 384], F32)
        hsc2 = sbuf("hsc2", [128, NFC], F32)
        osch = sbuf("osch", [128, OCN], F32)
        rbb = sbuf("rbb", [128, 256], F32)
        fbTf = sbuf("fbTf", [16, R], F32)
        fbTg = sbuf("fbTg", [16, R], BF16)
        gatef = sbuf("gatef", [1, 1], F32)
        g2c = sbuf("g2c", [16, 1], F32)
        lnscr = sbuf("lnscr", [128, 4096], BF16)
        zrow = sbuf("zrow", [1, 256], BF16)
        cvec = sbuf("cvecs", [128, 8], F32)
        cvecb = sbuf("cvecb", [128, 8], BF16)
        crow = sbuf("crows", [1, 128], F32)
        identb = sbuf("identb", [128, 128], BF16)
        xact = sbuf("xact", [128, DCN * NACT], F32)
        nmb = sbuf("nmb", [128, 2], F32)

        pbk = [octx.enter_context(nc.psum_tensor(f"pb{i}", [128, 512], F32))
               for i in range(7)]
        pt = octx.enter_context(nc.psum_tensor("pt", [128, 1024], BF16))
        ST = pbk[6]
        BC = pbk[2]

        ones_b = cvecb[:, 0:1]
        negones_b = cvecb[:, 5:6]
        eps_p1 = cvec[0:1, 3:4]
        ptf = pt[:, 0:1024].bitcast(F32)
        idf1 = cvec[0:1, 0:1]
        v3 = lambda t, w: t[:].rearrange("p (c t) -> p c t", t=w)
        lnscr3 = v3(lnscr, 512)
        sfc3 = v3(s_fc, HF)
        spj3 = v3(s_pj, D)
        stg3a = stg[:, 0:1024].rearrange("p (c t) -> p c t", c=DCN)
        stg3b = stg[:, 1024:2048].rearrange("p (c t) -> p c t", c=DCN)

        def emit_ln_gen(src3, dst3, W, src_wait=(), precast=False, scr=None,
                        scr_key="lnscr", dst_key=None, fscr3=None,
                        cast_split=False, stat_waits=(), out=None,
                        t3_key=None):
            scr = lnscr if scr is None else scr
            scr3 = v3(scr, 512)
            cast_v = None
            if not precast:
                for sem, val in src_wait:
                    E.wait("scalar", sem, val)
                if cast_split:
                    E.gwait2(("scalar", "vector"), scr_key)
                    E.op("scalar", "activation", scr3[:, 0:4, 0:W],
                         src3[:, 0:4, :], AF.Copy)
                    E.op("scalar", "activation", scr3[:, 0:4, 256:256 + W],
                         src3[:, 0:4, :], AF.Square, inc="sA")
                    for sem, val in src_wait:
                        E.wait("vector", sem, val)
                    E.op("vector", "tensor_copy", scr3[:, 4:8, 0:W],
                         src3[:, 4:8, :])
                    E.op("vector", "tensor_tensor", scr3[:, 4:8, 256:256 + W],
                         src3[:, 4:8, :], src3[:, 4:8, :], ALU.mult,
                         inc="sV")
                    cast_v = E.cnt["sV"]
                else:
                    E.gwait("scalar", scr_key)
                    E.op("scalar", "activation", scr3[:, :, 0:W], src3,
                         AF.Copy)
                    E.op("scalar", "activation", scr3[:, :, 256:256 + W],
                         src3, AF.Square, inc="sA")
            cA = E.cnt["sA"]
            yield "casts"
            E.gwait("tensor", "ST")
            E.wait("tensor", "sA", cA)
            if cast_v is not None:
                E.wait("tensor", "sV", cast_v)
            for sem, val in stat_waits:
                E.wait("tensor", sem, val)
            for dc in range(DCN):
                E.op("tensor", "matmul", ST[0:1, 0:512], ones_b,
                     scr[:, dc * 512: dc * 512 + 512],
                     start=(dc == 0), stop=(dc == DCN - 1),
                     inc=("sP" if dc == DCN - 1 else None))
            cP = E.cnt["sP"]
            E.gset(scr_key, "sP")
            yield "stats"
            E.gwait("vector", "srow")
            E.wait("vector", "sP", cP)
            E.op("vector", "tensor_scalar", srow[0:1, 0:W], ST[0:1, 0:W],
                 1.0 / D, None, ALU.mult)
            E.op("vector", "tensor_scalar", srow[0:1, 256:256 + W],
                 ST[0:1, 256:256 + W], 1.0 / D, None, ALU.mult, inc="sV")
            E.gset("ST", "sV")
            E.drain("vector")
            E.op("vector", "tensor_mul", srow[0:1, 512:512 + W],
                 srow[0:1, 0:W], srow[0:1, 0:W])
            E.drain("vector")
            E.op("vector", "tensor_sub", srow[0:1, 256:256 + W],
                 srow[0:1, 256:256 + W], srow[0:1, 512:512 + W], inc="sV")
            cV1 = E.cnt["sV"]
            E.wait("scalar", "sV", cV1)
            E.op("scalar", "activation", srow[0:1, 512:512 + W],
                 srow[0:1, 256:256 + W], AF.Ln, bias=eps_p1)
            E.drain("scalar")
            E.op("scalar", "activation", srow[0:1, 256:256 + W],
                 srow[0:1, 512:512 + W], AF.Exp, scale=-0.5, inc="sA")
            cA2 = E.cnt["sA"]
            E.wait("vector", "sA", cA2)
            E.op("vector", "tensor_mul", srow[0:1, 512:512 + W],
                 srow[0:1, 0:W], srow[0:1, 256:256 + W], inc="sV")
            cV2 = E.cnt["sV"]
            yield "rows"
            E.wait("tensor", "sV", cV2)
            E.op("tensor", "matmul", ST[:, 0:W], crow[:],
                 srow[0:1, 256:256 + W], start=True, stop=True)
            E.op("tensor", "matmul", ST[:, 256:256 + W], crow[:],
                 srow[0:1, 512:512 + W], start=True, stop=True, inc="sP")
            cPb = E.cnt["sP"]
            E.gset("srow", "sP")
            yield "bcast"
            E.wait("vector", "sP", cPb)
            if dst_key is not None:
                E.gwait("vector", dst_key)
            if t3_key is not None:
                E.gwait("vector", t3_key)
            rb = ST[:, 0:W].unsqueeze(1).broadcast_to((128, DCN, W))
            mrb = ST[:, 256:256 + W].unsqueeze(1).broadcast_to((128, DCN, W))
            t3 = dst3 if fscr3 is None else fscr3
            E.op("vector", "tensor_tensor", t3, src3, rb, ALU.mult)
            E.drain("vector")
            yield "apply1"
            E.op("vector", "tensor_tensor", dst3, t3, mrb, ALU.subtract,
                 inc="sV")
            E.gset("ST", "sV")
            if out is not None:
                out[0] = E.cnt["sV"]

        def emit_ln(*args, **kwargs):
            out = [None]
            for _ in emit_ln_gen(*args, out=out, **kwargs):
                pass
            return out[0]

        def adv(g):
            try:
                next(g)
            except StopIteration:
                pass

        for dst, src in [(cvec, t_cvec), (crow, t_crow), (fbTf, t_fbT),
                         (gatef, t_gate)]:
            E.dma("sync", dst[:], src[:, :], "dIn")
        E.dma("sync", v3(xact, NACT), t_x[:, NIN:TOK].rearrange(
            "(c p) t -> p c t", p=128), "dIn")
        c_dS_init = E.cnt["dIn"]

        E.dma("gpsimd", cvecb[:], t_cvec[:, :], "dGi")
        E.dma("gpsimd", identb[:], t_id[:, :], "dGi")
        E.dma("gpsimd", u2T[:], t_uT[:, :], "dGi")
        c_dG_init = E.cnt["dGi"]

        E.op("vector", "memset", zrow[:], 0.0)
        E.op("vector", "memset", lnscr[:], 0.0, inc="sV")
        c_memset = E.cnt["sV"]

        E.wait("tensor", "dIn", c_dS_init)
        E.wait("tensor", "dGi", c_dG_init)
        E.gwait("tensor", "ST")
        E.op("tensor", "matmul", ST[0:16, 0:1], crow[0:1, 0:16], gatef[:],
             start=True, stop=True, inc="sP")
        c_gb = E.cnt["sP"]
        E.wait("scalar", "sP", c_gb)
        E.op("scalar", "copy", g2c[:], ST[0:16, 0:1])
        E.drain("scalar")
        E.op("scalar", "activation", fbTg[:], fbTf[:], AF.Copy,
             scale=g2c[0:16, 0:1], inc="sA")
        c_fbTg = E.cnt["sA"]
        E.gset("ST", "sA")

        with ExitStack() as actx:
            asb = lambda n, s, dt: actx.enter_context(nc.sbuf_tensor(n, s, dt))
            wbuf = asb("wbuf", [128, DCN * 512], BF16)
            ctxXg = asb("ctxXg", [128, DCN * 64], F32)
            ctxn = asb("ctxn", [128, DCN * CTX], BF16)
            Kt = asb("Kt", [128, DCN * CTX], BF16)
            Qt = asb("Qt", [128, DCN * NACT], BF16)
            xnq = asb("xnq", [128, DCN * NACT], BF16)
            Sm = asb("Sm", [128, CTX], F32)
            AO = asb("AO", [128, DCN * NACT], F32)
            vnb = asb("vnb", [16, D], BF16)
            mwAs = asb("mwAs", [128, WIN], BF16)
            mwBs = asb("mwBs", [1, WIN], BF16)
            msAs = asb("msAs", [128, SINKS], BF16)
            msBs = asb("msBs", [1, SINKS], BF16)

            E.dma("gpsimd", vnb[:], t_vT[:, :], "dM")
            for dst, srct in [(mwAs, t_mwA), (mwBs, t_mwB), (msAs, t_msA),
                              (msBs, t_msB)]:
                E.dma("gpsimd", dst[:], srct[:, :], "dM")
            c_dG_masks = E.cnt["dM"]

            ctxn3 = v3(ctxn, CTX)
            Kt3 = v3(Kt, CTX)
            Qt3 = v3(Qt, NACT)
            xnq3 = v3(xnq, NACT)
            xact3 = v3(xact, NACT)
            AO3 = v3(AO, NACT)
            wbuf3 = v3(wbuf, 512)
            ctxXg3 = v3(ctxXg, 64)

            P_A = lnscr[:, 0:CTX]
            PTs = lnscr[:, 520:1160]
            PTb = lnscr[:, 1160:1165]
            Vh = lnscr[:, 1168:1168 + 5 * 64]
            O_s = lnscr[:, 1492:1492 + DCN * NACT]
            O_s3 = O_s.rearrange("p (c t) -> p c t", c=DCN)
            P_B = lnscr[0:1, 2524:2524 + CTX]
            plo = lnscr[:, 3072:4096]

            gsz = [64] * 8 + [4]
            gof = [64 * k for k in range(8)] + [512]
            for g in range(9):
                W = gsz[g]
                E.gwait("sync", "ctxXg")
                cL = E.dma("sync", ctxXg3[:, :, 0:W],
                           t_ctx[:, gof[g]:gof[g] + W].rearrange(
                               "(c p) t -> p c t", p=128), "dC")
                wl = [("dC", cL)] + ([("sV", c_memset)] if g == 0 else [])
                emit_ln(ctxXg3[:, :, 0:W], ctxn3[:, :, gof[g]:gof[g] + W], W,
                        src_wait=wl,
                        fscr3=Sm[:, 0:512].rearrange(
                            "p (c t) -> p c t", c=DCN)[:, :, 0:W])
                E.gset("ctxXg", "sV")
            c_ctxn = E.cnt["sV"]

            emit_ln(xact3, xnq3, NACT, src_wait=[("dIn", c_dS_init)],
                    fscr3=AO3)
            c_xnq = E.cnt["sV"]

            def load_wbuf(t_w, half):
                E.gwait("gpsimd", "wbuf")
                return E.dma("gpsimd", wbuf[:],
                             t_w[half * 128:(half + 1) * 128, :], "dW")

            big_i = [0]

            def next_big():
                i = big_i[0] % 2
                big_i[0] += 1
                return pbk[i], f"pbig{i}"

            att_units = []
            cw_h = {}

            def mk_K(dc2):
              def unit():
                if dc2 == 0:
                    cw_h["w"] = load_wbuf(t_wk, 0)
                if dc2 == 4:
                    cw_h["w"] = load_wbuf(t_wk, 1)
                c_w = cw_h["w"]
                bank, key = next_big()
                E.gwait("tensor", key)
                if dc2 in (0, 4):
                    E.wait("tensor", "dW", c_w)
                if dc2 == 0:
                    E.wait("tensor", "sV", c_ctxn)
                for dci in range(DCN):
                    E.op("tensor", "matmul", bank[:, 0:512],
                         wbuf3[:, dci, (dc2 % 4) * 128:(dc2 % 4) * 128 + 128],
                         ctxn3[:, dci, 4:516],
                         start=(dci == 0), stop=(dci == DCN - 1))
                E.gwait("tensor", "pbproj0")
                for dci in range(DCN):
                    E.op("tensor", "matmul", BC[:, 0:4],
                         wbuf3[:, dci, (dc2 % 4) * 128:(dc2 % 4) * 128 + 128],
                         ctxn3[:, dci, 0:4],
                         start=(dci == 0), stop=(dci == DCN - 1),
                         inc=("sP" if dci == DCN - 1 else None))
                cP = E.cnt["sP"]
                E.wait("scalar", "sP", cP)
                E.op("scalar", "activation", Kt3[:, dc2, 4:516], bank[:, 0:512],
                     AF.Copy)
                E.op("scalar", "activation", Kt3[:, dc2, 0:4], BC[:, 0:4],
                     AF.Copy, inc="sA")
                E.gset(key, "sA")
                E.gset("pbproj0", "sA")
                if dc2 == 3 or dc2 == DCN - 1:
                    E.gset("wbuf", "sP")
              return unit

            for dc2 in range(DCN):
                att_units.append(mk_K(dc2))

            def mk_Q(dc2):
              def unit():
                if dc2 == 0:
                    cw_h["w"] = load_wbuf(t_wq, 0)
                if dc2 == 4:
                    cw_h["w"] = load_wbuf(t_wq, 1)
                c_w = cw_h["w"]
                bank, key = next_big()
                E.gwait("tensor", key)
                if dc2 in (0, 4):
                    E.wait("tensor", "dW", c_w)
                if dc2 == 0:
                    E.wait("tensor", "sV", c_xnq)
                for dci in range(DCN):
                    E.op("tensor", "matmul", bank[:, 0:NACT],
                         wbuf3[:, dci, (dc2 % 4) * 128:(dc2 % 4) * 128 + 128],
                         xnq3[:, dci, 0:NACT],
                         start=(dci == 0), stop=(dci == DCN - 1),
                         inc=("sP" if dci == DCN - 1 else None))
                cP = E.cnt["sP"]
                E.wait("scalar", "sP", cP)
                E.op("scalar", "activation", Qt3[:, dc2, :], bank[:, 0:NACT],
                     AF.Copy, inc="sA")
                E.gset(key, "sA")
                if dc2 == DCN - 1:
                    cw_h["cQ"] = E.cnt["sA"]
                if dc2 == 3 or dc2 == DCN - 1:
                    E.gset("wbuf", "sP")
              return unit

            for dc2 in range(DCN):
                att_units.append(mk_Q(dc2))

            ksz = [SINKS, 128, 128, 128, 128]
            kof = [0, 4, 132, 260, 388]

            def mk_H(h):
              def unit():
                if h == 0:
                    cw_h["w"] = load_wbuf(t_wv, 0)
                if h == 8:
                    cw_h["w"] = load_wbuf(t_wv, 1)
                c_w = cw_h["w"]
                dc = h // 2
                pbq = 64 * (h % 2)
                E.gwait("tensor", "pb3")
                if h in (0, 8):
                    E.wait("tensor", "dW", c_w)
                if h == 0:
                    E.wait("tensor", "sA", E.cnt["sA"] if False else cw_h["cQ"])
                for kc in range(5):
                    for dci in range(DCN):
                        E.op("tensor", "matmul",
                             pbk[3][0:ksz[kc], 64 * kc:64 * kc + 64],
                             ctxn3[:, dci, kof[kc]:kof[kc] + ksz[kc]],
                             wbuf3[:, dci, (h % 8) * 64:(h % 8) * 64 + 64],
                             start=(dci == 0), stop=(dci == DCN - 1),
                             inc=("sP" if (kc == 4 and dci == DCN - 1) else None))
                cPv = E.cnt["sP"]
                E.gwait("scalar", "Vh")
                E.wait("scalar", "sP", cPv)
                for kc in range(5):
                    E.op("scalar", "activation",
                         Vh[0:ksz[kc], 64 * kc:64 * kc + 64],
                         pbk[3][0:ksz[kc], 64 * kc:64 * kc + 64], AF.Copy,
                         inc=("sA" if kc == 4 else None))
                E.gset("pb3", "sA")
                c_vh = E.cnt["sA"]
                if h == 7:
                    E.gset("wbuf", "sP")
                bank, key = next_big()
                E.gwait("tensor", key)
                E.op("tensor", "matmul", bank[:, 0:512],
                     Qt3[pbq:pbq + 64, dc, 0:128], Kt3[pbq:pbq + 64, dc, 4:516],
                     start=True, stop=True)
                E.gwait("tensor", "pbproj0")
                E.op("tensor", "matmul", BC[:, 0:4],
                     Qt3[pbq:pbq + 64, dc, 0:128], Kt3[pbq:pbq + 64, dc, 0:4],
                     start=True, stop=True)
                E.gwait("tensor", "pb4")
                E.op("tensor", "matmul", pbk[4][0:1, 0:512],
                     Qt3[pbq:pbq + 64, dc, 128:129], Kt3[pbq:pbq + 64, dc, 4:516],
                     start=True, stop=True)
                E.op("tensor", "matmul", BC[0:1, 8:12],
                     Qt3[pbq:pbq + 64, dc, 128:129], Kt3[pbq:pbq + 64, dc, 0:4],
                     start=True, stop=True, inc="sP")
                cPs = E.cnt["sP"]
                E.gwait("vector", "Sm")
                if h == 0:
                    E.wait("vector", "dM", c_dG_masks)
                E.wait("vector", "sP", cPs)
                E.op("vector", "tensor_tensor", Sm[:, 4:516], bank[:, 0:512],
                     mwAs[:], ALU.add)
                E.op("vector", "tensor_tensor", Sm[:, 0:4], BC[:, 0:4],
                     msAs[:], ALU.add, inc="sV")
                E.gset(key, "sV")
                E.drain("vector")
                E.op("vector", "reduce_max", nmb[:, 0:1], Sm[:, 0:516],
                     axis=AX.X, negate=True)
                E.drain("vector")
                E.op("vector", "tensor_scalar", nmb[:, 0:1], nmb[:, 0:1],
                     0.125, None, ALU.mult, inc="sV")
                cVm = E.cnt["sV"]
                E.gwait("vector", "srow")
                E.op("vector", "tensor_tensor", srow[0:1, 4:516],
                     pbk[4][0:1, 0:512], mwBs[:], ALU.add)
                E.op("vector", "tensor_tensor", srow[0:1, 0:4],
                     BC[0:1, 8:12], msBs[:], ALU.add, inc="sV")
                E.gset("pb4", "sV")
                E.gset("pbproj0", "sV")
                E.drain("vector")
                E.op("vector", "reduce_max", srow[0:1, 520:521],
                     srow[0:1, 0:516], axis=AX.X, negate=True)
                E.drain("vector")
                E.op("vector", "tensor_scalar", srow[0:1, 520:521],
                     srow[0:1, 520:521], 0.125, None, ALU.mult, inc="sV")
                cVb = E.cnt["sV"]
                E.gwait("scalar", "P")
                E.wait("scalar", "sV", cVm)
                E.op("scalar", "activation", P_A[:], Sm[:, 0:516], AF.Exp,
                     scale=0.125, bias=nmb[:, 0:1], inc="sA")
                E.gset("Sm", "sA")
                E.wait("scalar", "sV", cVb)
                E.op("scalar", "activation", P_B[:], srow[0:1, 0:516], AF.Exp,
                     scale=0.125, bias=srow[0:1, 520:521], inc="sA")
                cAe = E.cnt["sA"]
                E.gwait("vector", "Pn")
                E.wait("vector", "sA", cAe)
                E.op("vector", "reduce_sum", nmb[:, 1:2], P_A[:], axis=AX.X)
                E.op("vector", "reduce_sum", srow[0:1, 521:522], P_B[:],
                     axis=AX.X)
                E.drain("vector")
                E.op("vector", "reciprocal", nmb[:, 1:2], nmb[:, 1:2])
                E.op("vector", "reciprocal", srow[0:1, 521:522],
                     srow[0:1, 521:522])
                E.drain("vector")
                E.op("vector", "tensor_scalar", P_A[:], P_A[:], nmb[:, 1:2],
                     None, ALU.mult)
                E.op("vector", "tensor_scalar", P_B[:], P_B[:],
                     srow[0:1, 521:522], None, ALU.mult, inc="sV")
                E.gset("srow", "sV")
                cVp = E.cnt["sV"]
                E.gwait("tensor", "pt")
                E.wait("tensor", "sV", cVp)
                for kc in range(5):
                    E.op("tensor", "transpose",
                         pt[0:ksz[kc], 128 * kc:128 * kc + 128],
                         P_A[:, kof[kc]:kof[kc] + ksz[kc]], identb[:])
                for kc in range(5):
                    E.op("tensor", "transpose",
                         pt[0:ksz[kc], 640 + 2 * kc:641 + 2 * kc],
                         P_B[0:1, kof[kc]:kof[kc] + ksz[kc]],
                         identb[0:1, 0:1],
                         inc=("sP" if kc == 4 else None))
                cPt = E.cnt["sP"]
                E.gset("Pn", "sP")
                E.gwait("scalar", "PTs")
                E.gwait("scalar", "PTb")
                E.wait("scalar", "sP", cPt)
                E.op("scalar", "activation", PTs[:, 128:640], pt[:, 128:640],
                     AF.Copy)
                E.op("scalar", "activation", PTs[0:SINKS, 0:128],
                     pt[0:SINKS, 0:128], AF.Copy)
                E.op("scalar", "activation", PTb[:, 1:5],
                     pt[:, 642:650].rearrange("p (k two) -> p k two",
                                              two=2)[:, :, 0:1],
                     AF.Copy)
                E.op("scalar", "activation", PTb[0:SINKS, 0:1],
                     pt[0:SINKS, 640:641], AF.Copy, inc="sA")
                E.gwait("scalar", "PTb") if False else None
                E.gset("pt", "sA")
                E.gset("P", "sA")
                cAt = E.cnt["sA"]
                roff = 130 * (dc % 2)
                if h % 2 == 0:
                    E.gwait("tensor", f"pO{dc % 2}")
                E.wait("tensor", "sA", cAt)
                E.wait("tensor", "sA", c_vh)
                for kc in range(5):
                    E.op("tensor", "matmul",
                         pbk[5][pbq:pbq + 64, roff:roff + 128],
                         Vh[0:ksz[kc], 64 * kc:64 * kc + 64],
                         PTs[0:ksz[kc], 128 * kc:128 * kc + 128],
                         start=(kc == 0), stop=(kc == 4))
                for kc in range(5):
                    E.op("tensor", "matmul",
                         pbk[5][pbq:pbq + 64, roff + 128:roff + 129],
                         Vh[0:ksz[kc], 64 * kc:64 * kc + 64],
                         PTb[0:ksz[kc], kc:kc + 1],
                         start=(kc == 0), stop=(kc == 4),
                         inc=("sP" if kc == 4 else None))
                cPo = E.cnt["sP"]
                E.gset("Vh", "sP")
                E.gset("PTs", "sP")
                E.gset("PTb", "sP")
                if h == NH - 1:
                    E.gset("wbuf", "sP")
                if h % 2 == 1:
                    E.wait("scalar", "sP", cPo)
                    E.op("scalar", "activation", O_s3[:, dc, :],
                         pbk[5][:, roff:roff + 129], AF.Copy, inc="sA")
                    E.gset(f"pO{dc % 2}", "sA")
              return unit

            for h in range(NH):
                att_units.append(mk_H(h))
            import os
            if n_tiles < 0:
                NFCx = NFC if n_tiles == -3 else 0
                NPJx = 0
                n_tiles = 0
            else:
                NFCx, NPJx = NFC, NPJ
            NFCx = int(os.environ.get("KB_NFC", NFCx))
            NPJx = int(os.environ.get("KB_NPJ", NPJx))
            sfc_ready = []
            hsc_ready = []
            spj_ready = []
            osch_ready = []

            def mk_prep(kind, idx):
              def gen():
                slot = (idx if kind == "fc" else NFCx + idx) % 2
                stflat = stg[:, slot * 1024:slot * 1024 + 1024]
                if kind == "fc":
                    red_c, red_w, inv_n = 4, 128, 1.0 / D
                    src_t = t_wfc[idx * 128:(idx + 1) * 128, :]
                    st3 = stg3a if slot == 0 else stg3b
                    cn = DCN
                else:
                    red_c, red_w, inv_n = 16, 32, 1.0 / HF
                    src_t = t_wpj[idx * 128:(idx + 1) * 128, :]
                    st3 = stflat.rearrange("p (c t) -> p c t", c=HCN)
                    cn = HCN
                sga = sgn1[:, 0:1024]
                sgb = sgn1[:, 1024:2048]
                E.gwait2(("sync", "scalar"), f"stg{slot}")
                cL = E.dma("sync", stflat[:, 0:512], src_t[:, 0:512],
                           f"dS{slot}")
                cL2 = E.dma("scalar", stflat[:, 512:1024],
                            src_t[:, 512:1024], "dC")
                E.gwait("scalar", "sgn1a")
                E.wait("scalar", f"dS{slot}", cL)
                E.wait("scalar", "dC", cL2)
                E.op("scalar", "activation", sga, stflat, AF.Relu,
                     scale=1.0, inc="sA")
                cHiA = E.cnt["sA"]
                E.gwait("vector", "plo")
                E.wait("vector", "sA", cHiA)
                E.op("vector", "scalar_tensor_tensor", plo[:, 0:1024],
                     stflat, 0.0, sga, ALU.max, ALU.subtract, inc="sV")
                cLoA = E.cnt["sV"]
                yield "A1a"
                E.wait("tensor", "sA", cHiA)
                E.wait("tensor", "sV", cLoA)
                E.gwait("tensor", "ST")
                E.gwait("tensor", "pb4")
                E.op("tensor", "matmul", ST[0:1, 0:512], ones_b,
                     sga[:, 0:512], start=True, stop=False)
                E.op("tensor", "matmul", pbk[4][0:1, 0:512], ones_b,
                     sga[:, 512:1024], start=True, stop=False)
                E.op("tensor", "matmul", ST[0:1, 0:512], ones_b,
                     plo[:, 0:512], start=False, stop=False)
                E.op("tensor", "matmul", pbk[4][0:1, 0:512], ones_b,
                     plo[:, 512:1024], start=False, stop=False, inc="sP")
                E.gset("plo", "sP")
                E.gset("sgn1a", "sP")
                E.gwait("scalar", "sgn1b")
                E.op("scalar", "activation", sgb, stflat, AF.Relu,
                     scale=-1.0, inc="sA")
                cHiB = E.cnt["sA"]
                E.gwait("vector", "plo")
                E.wait("vector", "sA", cHiB)
                E.op("vector", "scalar_tensor_tensor", plo[:, 0:1024],
                     stflat, 0.0, sgb, ALU.min, ALU.add, inc="sV")
                cLoB = E.cnt["sV"]
                E.wait("tensor", "sA", cHiB)
                E.wait("tensor", "sV", cLoB)
                E.op("tensor", "matmul", ST[0:1, 0:512], ones_b,
                     sgb[:, 0:512], start=False, stop=False)
                E.op("tensor", "matmul", pbk[4][0:1, 0:512], ones_b,
                     sgb[:, 512:1024], start=False, stop=False)
                E.op("tensor", "matmul", ST[0:1, 0:512], negones_b,
                     plo[:, 0:512], start=False, stop=True)
                E.op("tensor", "matmul", pbk[4][0:1, 0:512], negones_b,
                     plo[:, 512:1024], start=False, stop=True, inc="sP")
                E.gset("plo", "sP")
                E.gset("sgn1b", "sP")
                cSt = E.cnt["sP"]
                yield "A1b"
                E.gwait("vector", "srowp")
                E.wait("vector", "sP", cSt)
                v1 = ST[0:1, 0:512].rearrange("p (c h) -> p h c", c=red_c)
                v2 = pbk[4][0:1, 0:512].rearrange("p (c h) -> p h c",
                                                  c=red_c)
                E.op("vector", "tensor_reduce", srowp[0:1, 0:red_w], v1,
                     AX.X, ALU.add)
                E.op("vector", "tensor_reduce",
                     srowp[0:1, 256:256 + red_w], v2, AX.X, ALU.add,
                     inc="sV")
                E.gset("pb4", "sV")
                E.drain("vector")
                E.op("vector", "tensor_add", srowp[0:1, 0:red_w],
                     srowp[0:1, 0:red_w], srowp[0:1, 256:256 + red_w])
                E.drain("vector")
                E.op("vector", "tensor_scalar", srowp[0:1, 0:red_w],
                     srowp[0:1, 0:red_w], inv_n, 1e-5, ALU.mult, ALU.max)
                E.drain("vector")
                E.op("vector", "reciprocal", srowp[0:1, 128:128 + red_w],
                     srowp[0:1, 0:red_w], inc="sV")
                cRw = E.cnt["sV"]
                yield "A2a"
                if kind == "fc":
                    E.wait("scalar", "sV", cRw)
                    E.op("scalar", "activation", srowp[0:1, 256:384],
                         srowp[0:1, 0:128], AF.Square, inc="sA")
                    cH2 = E.cnt["sA"]
                    E.wait("tensor", "sA", cH2)
                    E.op("tensor", "matmul", ST[:, 128:256], crow[:],
                         srowp[0:1, 128:256], start=True, stop=True)
                    E.op("tensor", "transpose", ST[:, 384:385],
                         srowp[0:1, 256:384], idf1, inc="sP")
                    cBc = E.cnt["sP"]
                    E.gset("srowp", "sP")
                    E.wait("scalar", "sP", cBc)
                    E.gwait("scalar", f"rbb{slot}")
                    E.op("scalar", "activation", hsc2[:, idx:idx + 1],
                         ST[:, 384:385], AF.Copy, inc="sA")
                    hsc_ready.append(E.cnt["sA"])
                    E.op("scalar", "activation",
                         rbb[:, slot * 128:slot * 128 + 128],
                         ST[:, 128:256], AF.Copy, inc="sA")
                else:
                    E.wait("tensor", "sV", cRw)
                    E.op("tensor", "matmul", ST[:, 128:160], crow[:],
                         srowp[0:1, 128:160], start=True, stop=True)
                    E.op("tensor", "transpose", ST[0:32, 384:385],
                         srowp[0:1, 0:32], idf1, inc="sP")
                    cBc = E.cnt["sP"]
                    E.gset("srowp", "sP")
                    E.wait("scalar", "sP", cBc)
                    E.gwait("scalar", f"rbb{slot}")
                    oc = idx // 4
                    rlo = 32 * (idx % 4)
                    E.op("scalar", "activation",
                         osch[rlo:rlo + 32, oc:oc + 1],
                         ST[0:32, 384:385], AF.Copy, inc="sA")
                    osch_ready.append(E.cnt["sA"])
                    E.op("scalar", "activation",
                         rbb[:, slot * 128:slot * 128 + 32],
                         ST[:, 128:160], AF.Copy, inc="sA")
                cRb = E.cnt["sA"]
                E.gset("ST", "sA")
                yield "A2b"
                E.wait("vector", "sA", cRb)
                w = 128 if kind == "fc" else 32
                rb3 = rbb[:, slot * 128:slot * 128 + w].unsqueeze(
                    1).broadcast_to((128, cn, w))
                if kind == "fc":
                    E.op("vector", "tensor_tensor", st3, st3, rb3, ALU.mult)
                    if idx == 0:
                        E.wait("vector", "dGi", c_dG_init)
                    E.op("vector", "tensor_tensor",
                         u2T[0:16, idx * 128:(idx + 1) * 128],
                         u2T[0:16, idx * 128:(idx + 1) * 128],
                         rbb[0:16, slot * 128:slot * 128 + 128], ALU.mult,
                         inc="sV")
                else:
                    E.op("vector", "tensor_tensor", st3, st3, rb3, ALU.mult,
                         inc="sV")
                E.gset(f"rbb{slot}", "sV")
                E.drain("vector")
                E.gwait("vector", "sgn1b")
                E.op("vector", "tensor_single_scalar", sgb, stflat, -0.5,
                     ALU.is_lt, inc="sV")
                E.drain("vector")
                if kind == "fc":
                    E.op("vector", "scalar_tensor_tensor",
                         sfc3[:, :, idx * 128:(idx + 1) * 128], st3, 0.5,
                         sgb.rearrange("p (c t) -> p c t", c=DCN),
                         ALU.is_gt, ALU.subtract, inc="sV")
                    sfc_ready.append(E.cnt["sV"])
                else:
                    E.op("vector", "scalar_tensor_tensor",
                         spj3[:, :, idx * 32:(idx + 1) * 32], st3, 0.5,
                         sgb.rearrange("p (c t) -> p c t", c=HCN),
                         ALU.is_gt, ALU.subtract, inc="sV")
                    spj_ready.append(E.cnt["sV"])
                E.gset(f"stg{slot}", "sV")
                E.gset("sgn1b", "sV")
              return gen()

            prep_units = []
            for ph in range(NFCx):
                prep_units.append(("fc", ph))
            for po in range(NPJx):
                prep_units.append(("pj", po))

            pi = 0
            prev_g = [None]

            cur_g = [None]

            def prep_step():
                nonlocal pi
                if cur_g[0] is None:
                    if pi >= len(prep_units):
                        return
                    cur_g[0] = mk_prep(*prep_units[pi])
                    pi += 1
                    next(cur_g[0])
                g = cur_g[0]
                if prev_g[0] is not None:
                    adv(prev_g[0])
                next(g)
                next(g)
                if pi < len(prep_units):
                    cur_g[0] = mk_prep(*prep_units[pi])
                    pi += 1
                    next(cur_g[0])
                else:
                    cur_g[0] = None
                next(g)
                prev_g[0] = g

            for ui, unit in enumerate(att_units):
                prep_step()
                unit()
                prep_step()
            if prev_g[0] is not None:
                adv(prev_g[0])
            while pi < len(prep_units):
                prep_units[pi]()
                pi += 1
            c_u2T = sfc_ready[-1] if sfc_ready else 0
            c_spj = spj_ready[-1] if spj_ready else 0

            c_wlr = 0
            vfbT = Kt[0:16, 0:1024]
            if NFCx == NFC:
                E.gwait("tensor", "ST")
                E.wait("tensor", "sA", c_fbTg)
                E.wait("tensor", "dM", c_dG_masks)
                for hf2 in range(2):
                    if hf2 == 1:
                        E.wait("tensor", "sA", E.cnt["sA"])
                    E.op("tensor", "matmul", ST[0:16, 0:512], fbTg[:],
                         vnb[0:16, hf2 * 512:hf2 * 512 + 512],
                         start=True, stop=True, inc="sP")
                    cM1 = E.cnt["sP"]
                    E.wait("scalar", "sP", cM1)
                    E.op("scalar", "activation",
                         vfbT[0:16, hf2 * 512:hf2 * 512 + 512],
                         ST[0:16, 0:512], AF.Copy, inc="sA")
                cV1 = E.cnt["sA"]
                E.gset("ST", "sA")
                E.wait("tensor", "sA", cV1)
                E.wait("tensor", "sV", c_u2T)
                for dc in range(DCN):
                    for hh in range(DCN):
                        bank, key = next_big()
                        E.gwait("tensor", key)
                        E.op("tensor", "matmul", bank[:, 0:512],
                             vfbT[0:16, dc * 128:dc * 128 + 128],
                             u2T[0:16, hh * 512:hh * 512 + 512],
                             start=True, stop=True, inc="sP")
                        cX = E.cnt["sP"]
                        sl = s_fc[:, dc * HF + hh * 512:dc * HF + hh * 512 + 512]
                        E.wait("vector", "sP", cX)
                        E.op("vector", "tensor_tensor", sl, sl, bank[:, 0:512],
                             ALU.add, inc="sV")
                        E.gset(key, "sV")
                c_wlr = E.cnt["sV"]
            c_O = E.cnt["sA"]

            c_w = load_wbuf(t_wo, 0)
            for oc2 in range(DCN):
                if oc2 == 4:
                    c_w = load_wbuf(t_wo, 1)
                bank, key = next_big()
                E.gwait("tensor", key)
                if oc2 in (0, 4):
                    E.wait("tensor", "dW", c_w)
                if oc2 == 0:
                    E.wait("tensor", "sA", c_O)
                for dci in range(DCN):
                    E.op("tensor", "matmul", bank[:, 0:NACT],
                         wbuf3[:, dci, (oc2 % 4) * 128:(oc2 % 4) * 128 + 128],
                         O_s3[:, dci, 0:NACT],
                         start=(dci == 0), stop=(dci == DCN - 1),
                         inc=("sP" if dci == DCN - 1 else None))
                cP = E.cnt["sP"]
                E.wait("vector", "sP", cP)
                E.op("vector", "tensor_copy", AO3[:, oc2, :], bank[:, 0:NACT],
                     inc="sV")
                E.gset(key, "sV")
                if oc2 == 3 or oc2 == DCN - 1:
                    E.gset("wbuf", "sP")
            c_AO = E.cnt["sV"]
            E.gset("lnscr", "sP")

            if n_tiles >= 0 or n_tiles in (-2, -3):
                emit_ln(AO3, AO3, NACT, src_wait=[("sV", c_AO)])
                E.drain("vector")
                E.op("vector", "tensor_tensor", xact3, xact3, AO3, ALU.add,
                     inc="sV")
            c_attn = E.cnt["sV"]



        with ExitStack() as bctx:
            bsb = lambda n, s, dt: bctx.enter_context(nc.sbuf_tensor(n, s, dt))
            xtl = [bsb(f"xtl{i}", [128, DCN * TT], F32) for i in range(2)]
            xn_t = [bsb(f"xn{i}", [128, DCN * TT], BF16) for i in range(2)]
            fsq = bsb("fsq", [128, DCN * 512], BF16)
            fscr = stg
            E.gset("fscr", "sV")
            r2s = [bsb(f"r2s{i}", [128, TT], BF16) for i in range(6)]
            rpp = [bsb(f"rpp{i}", [128, TT], F32) for i in range(2)]

            xn3_t = [v3(x, TT) for x in xn_t]
            fsq3 = v3(fsq, 512)
            fscr3 = v3(fscr, TT)
            xact3 = v3(xact, NACT)


            xload = {}

            def emit_xload(j):
                if j >= n_tiles:
                    return
                if j == 0:
                    E.wait("sync", "sV", c_attn)
                    E.wait("gpsimd", "sV", c_attn)
                if j < 2:
                    E.wait("sync", f"dS{j % 2}", E.cnt[f"dS{j % 2}"])
                E.gwait2(("sync", "gpsimd"), f"xtl{j % 2}")
                xv = v3(xtl[j % 2], TT)
                ca = E.dma("sync", xv[:, 0:4, :],
                           t_x[0:512, j * TT:(j + 1) * TT].rearrange(
                               "(c p) t -> p c t", p=128), f"dS{j % 2}")
                cb = E.dma("gpsimd", xv[:, 4:8, :],
                           t_x[512:1024, j * TT:(j + 1) * TT].rearrange(
                               "(c p) t -> p c t", p=128),
                           "dGi" if j % 2 == 0 else "dM")
                xload[j] = (ca, cb)

            emit_xload(0)
            emit_xload(1)

            c_store = []

            def ln1_start(jj):
                xfj3 = v3(xtl[jj % 2], TT)
                merge_wait = []
                if jj == n_tiles - 1 and n_tiles == NTILE:
                    E.wait("vector", f"dS{jj % 2}", xload[jj][0])
                    E.wait("vector", "dGi" if jj % 2 == 0 else "dM",
                           xload[jj][1])
                    E.wait("vector", "sV", c_attn)
                    E.op("vector", "tensor_copy", xfj3[:, :, TT - NACT:TT],
                         xact3, inc="sV")
                    merge_wait = [("sV", E.cnt["sV"])]
                    E.drain("vector")
                out = [None]
                g = emit_ln_gen(xfj3, xn3_t[jj % 2], TT,
                                src_wait=[(f"dS{jj % 2}", xload[jj][0]),
                                          ("dGi" if jj % 2 == 0 else "dM",
                                           xload[jj][1])]
                                + merge_wait,
                                dst_key=f"xn{jj % 2}", fscr3=fscr3,
                                cast_split=True, out=out, t3_key="fscr")
                next(g)
                return g, out

            r2_of = {}
            tile_state = {}

            def emit_fc(j, hc):
                bank = pbk[hc % 2]
                key = f"pbig{hc % 2}"
                E.gwait("tensor", key)
                for dci in range(DCN):
                    E.op("tensor", "matmul", bank[:, 0:TT],
                         s_fc[:, dci * HF + hc * 128:dci * HF + hc * 128 + 128],
                         xn_t[j % 2][:, dci * TT:(dci + 1) * TT],
                         start=(dci == 0), stop=(dci == DCN - 1),
                         inc=("sP" if dci == DCN - 1 else None))
                cF = E.cnt["sP"]
                if hc % 2 == 0:
                    E.gwait("scalar", f"rpp{hc % 2}")
                    E.wait("scalar", "sP", cF)
                    if j == 0 and hc == 0:
                        E.wait("scalar", "sA", hsc_ready[-1])
                    E.op("scalar", "activation", rpp[hc % 2][:],
                         bank[:, 0:TT], AF.Relu,
                         scale=hsc2[:, hc:hc + 1], inc="sA")
                    cR = E.cnt["sA"]
                    E.gwait("vector", f"r2s{hc % 6}")
                    E.wait("vector", "sP", cF)
                    E.wait("vector", "sA", cR)
                else:
                    E.gwait("vector", f"rpp{hc % 2}")
                    E.wait("vector", "sP", cF)
                    E.op("vector", "tensor_scalar", rpp[hc % 2][:],
                         bank[:, 0:TT], hsc2[:, hc:hc + 1], 0.0,
                         ALU.mult, ALU.max)
                    E.drain("vector")
                    E.gwait("vector", f"r2s{hc % 6}")
                E.op("vector", "tensor_tensor", r2s[hc % 6][:],
                     bank[:, 0:TT], rpp[hc % 2][:], ALU.mult, inc="sV")
                r2_of[j][hc] = E.cnt["sV"]
                E.gset(key, "sV")
                E.gset(f"rpp{hc % 2}", "sV")

            def emit_proj(j, hc):
                E.wait("tensor", "sV", r2_of[j][hc])
                for oc in range(OCN):
                    bank = pbk[2 + oc // 2]
                    E.op("tensor", "matmul",
                         bank[:, (oc % 2) * TT:(oc % 2) * TT + TT],
                         s_pj[:, hc * D + oc * 128:hc * D + oc * 128 + 128],
                         r2s[hc % 6][:], start=False,
                         stop=(hc == HCN - 1 and oc % 2 == 1),
                         inc=("sP" if oc == OCN - 1 else None))
                E.gset(f"r2s{hc % 6}", "sP")

            def fc_head(j):
                r2_of[j] = [0] * HCN
                E.wait("tensor", "sV", tile_state.pop(j))
                if j == 0:
                    E.wait("tensor", "sV", c_wlr)
                    E.wait("tensor", "sV", c_u2T)
                for hc in range(4):
                    emit_fc(j, hc)

            def ln2_tail(p, g2):
                xfp3 = v3(xtl[p % 2], TT)
                E.drain("vector")
                E.op("vector", "tensor_tensor", fscr3, xfp3, fscr3, ALU.add,
                     inc="sV")
                cOut = E.cnt["sV"]
                E.gset(f"xtl{p % 2}", "sV")
                E.drain("vector")
                E.wait("scalar", "sV", cOut)
                cS = E.dma("scalar",
                           t_out[:, p * TT:(p + 1) * TT].rearrange(
                               "(c p) t -> p c t", p=128), fscr3, f"dO{p % 2}")
                c_store.append((f"dO{p % 2}", cS))
                E.gset("fscr", f"dO{p % 2}")

            if n_tiles > 0:
                g0, o0 = ln1_start(0)
                for _ in g0:
                    pass
                E.gset("fscr", "sV")
                tile_state[0] = o0[0]
                fc_head(0)

            pend2 = None
            ln1 = None
            for j in range(n_tiles):
                emit_fc(j, 4)
                emit_fc(j, 5)
                if pend2 is not None:
                    adv(pend2)
                for bnk in range(4):
                    E.gwait("tensor", f"pbproj{bnk}")
                    if j == 0 and bnk == 0:
                        E.wait("tensor", "sV", c_spj)
                    E.op("tensor", "matmul", pbk[2 + bnk][:, 0:256],
                         zrow[0:1, 0:128], zrow[0:1, 0:256],
                         start=True, stop=False)
                    E.op("tensor", "matmul", pbk[2 + bnk][:, 256:512],
                         zrow[0:1, 0:128], zrow[0:1, 0:256],
                         start=False, stop=False)
                for hc in range(6, HCN):
                    emit_proj(j, hc - 6)
                    emit_fc(j, hc)
                    if pend2 is not None:
                        if hc == 7:
                            adv(pend2)
                        elif hc == 9:
                            adv(pend2)
                        elif hc == 11:
                            adv(pend2)
                        elif hc == 13:
                            adv(pend2)
                        elif hc == 15:
                            ln2_tail(j - 1, pend2)
                            pend2 = None
                    if hc == 15 and j >= 1:
                        emit_xload(j + 1)
                    if j + 1 < n_tiles:
                        if hc == 18:
                            ln1 = ln1_start(j + 1)
                        elif hc == 23:
                            adv(ln1[0])
                        elif hc == 26:
                            adv(ln1[0])
                        elif hc == 27:
                            adv(ln1[0])
                        elif hc == 28:
                            adv(ln1[0])
                        elif hc == 29:
                            adv(ln1[0])
                            E.gset("fscr", "sV")
                            tile_state[j + 1] = ln1[1][0]
                E.gset(f"xn{j % 2}", "sP")
                for hc in range(HCN - 6, HCN):
                    emit_proj(j, hc)
                c_proj = E.cnt["sP"]
                if j + 1 < n_tiles:
                    fc_head(j + 1)
                E.gwait2(("scalar", "vector"), "fsq")
                E.gwait2(("scalar", "vector"), "fscr")
                E.wait("scalar", "sP", c_proj)
                if j == 0:
                    E.wait("scalar", "sA", osch_ready[-1])
                for oc in range(4):
                    bank = pbk[2 + oc // 2]
                    reg = bank[:, (oc % 2) * TT:(oc % 2) * TT + TT]
                    E.op("scalar", "activation", fsq3[:, oc, 0:TT], reg,
                         AF.Copy, scale=osch[:, oc:oc + 1])
                    E.op("scalar", "activation", fsq3[:, oc, 256:256 + TT],
                         reg, AF.Square, scale=osch[:, oc:oc + 1])
                    E.op("scalar", "activation", fscr3[:, oc, :], reg,
                         AF.Copy, scale=osch[:, oc:oc + 1],
                         inc=("sA" if oc == 3 else None))
                cC2a = E.cnt["sA"]
                E.gset("pbproj0", "sA")
                E.gset("pbproj1", "sA")
                E.wait("vector", "sP", c_proj)
                for oc in range(4, OCN):
                    bank = pbk[2 + oc // 2]
                    reg = bank[:, (oc % 2) * TT:(oc % 2) * TT + TT]
                    E.op("vector", "tensor_scalar", fscr3[:, oc, :], reg,
                         osch[:, oc:oc + 1], None, ALU.mult,
                         inc=("sV" if oc == OCN - 1 else None))
                E.gset("pbproj2", "sV")
                E.gset("pbproj3", "sV")
                E.drain("vector")
                for oc in range(4, OCN):
                    E.op("vector", "tensor_copy", fsq3[:, oc, 0:TT],
                         fscr3[:, oc, :])
                    E.op("vector", "tensor_tensor",
                         fsq3[:, oc, 256:256 + TT], fscr3[:, oc, :],
                         fscr3[:, oc, :], ALU.mult,
                         inc=("sV" if oc == OCN - 1 else None))
                cC2v = E.cnt["sV"]
                o2 = [None]
                g2 = emit_ln_gen(fscr3, fscr3, TT, precast=True, scr=fsq,
                                 scr_key="fsq",
                                 stat_waits=[("sV", cC2v)], out=o2)
                next(g2)
                pend2 = g2

            if pend2 is not None:
                adv(pend2)
                adv(pend2)
                adv(pend2)
                adv(pend2)
                adv(pend2)
                ln2_tail(n_tiles - 1, pend2)

            E.wait("scalar", "dO0", E.cnt["dO0"])
            E.wait("scalar", "dO1", E.cnt["dO1"])

        sems = {}
        with ExitStack() as sctx:
            for s in ("dIn", "dS0", "dS1", "dO0", "dO1", "dGi", "dM", "dC",
                      "dW", "sP", "sA", "sV", "sG"):
                sems[s] = sctx.enter_context(nc.semaphore(s))
            block = sctx.enter_context(nc.Block())

            @block.sync
            def _(sync):
                E.replay("sync", sync, sems)

            @block.gpsimd
            def _(gp):
                E.replay("gpsimd", gp, sems)

            @block.tensor
            def _(te):
                E.replay("tensor", te, sems)

            @block.scalar
            def _(sc):
                E.replay("scalar", sc, sems)

            @block.vector
            def _(ve):
                E.replay("vector", ve, sems)

    return nc


def _stage_inputs(L0_attn_x, Wq, Wk, Wv, Wo, fast_b, w_fc, w_proj, u, v, gate):
    x = np.ascontiguousarray(np.asarray(L0_attn_x, dtype=np.float32))
    def _stw(W):
        W = np.asarray(W, np.float32)
        return np.ascontiguousarray(
            W.reshape(DCN, 128, 2, 512).transpose(2, 1, 0, 3).reshape(
                256, DCN * 512))

    shared = {
        "wq": _stw(Wq),
        "wk": _stw(Wk),
        "wv": _stw(Wv),
        "wo": _stw(Wo),
        "uT": np.ascontiguousarray(np.asarray(u, np.float32).T),
        "vT": np.ascontiguousarray(np.asarray(v, np.float32)),
        "fbT": np.ascontiguousarray(np.asarray(fast_b, np.float32).T),
        "gate": np.asarray(gate, np.float32).reshape(1, 1),
        "cvec": np.tile(np.array([[1.0, 0.5, -0.5, EPS, 2.0, -1.0, 0, 0]],
                                 np.float32), (128, 1)),
        "crow": np.ones((1, 128), np.float32),
        "ident": np.eye(128, dtype=np.float32),
    }
    wfcT = np.asarray(w_fc, np.float32).T
    shared["wfc_st"] = np.ascontiguousarray(
        wfcT.reshape(DCN, 128, NFC, FCP).transpose(2, 1, 0, 3).reshape(
            NFC * 128, DCN * FCP))
    wpjT = np.asarray(w_proj, np.float32).T
    shared["wpj_st"] = np.ascontiguousarray(
        wpjT.reshape(HCN, 128, NPJ, PJP).transpose(2, 1, 0, 3).reshape(
            NPJ * 128, HCN * PJP))

    in_maps = []
    for core in range(8):
        beta, i = divmod(core, CPB)
        m = dict(shared)
        inact = x[beta, NIN * i:NIN * (i + 1)]
        act = x[beta, ACT0 + NACT * i:ACT0 + NACT * (i + 1)]
        shard = np.concatenate([inact, act], axis=0)
        m["xT"] = np.ascontiguousarray(shard.T)
        ctx = np.concatenate([x[beta, 0:SINKS], x[beta, T - WIN:T]], axis=0)
        m["ctxT"] = np.ascontiguousarray(ctx.T)
        rr = ACT0 + NACT * i + np.arange(NACT)[:, None]
        jw = np.arange(WIN)[None, :]
        mw = np.where(jw <= rr - (T - WIN), 0.0, NEG).astype(np.float32)
        cs = np.arange(SINKS)[None, :]
        ms = np.where(cs <= rr - ACT0, 0.0, NEG).astype(np.float32)
        m["mwA"] = np.ascontiguousarray(mw[0:128])
        m["mwB"] = np.ascontiguousarray(mw[128:129])
        m["msA"] = np.ascontiguousarray(ms[0:128])
        m["msB"] = np.ascontiguousarray(ms[128:129])
        in_maps.append(m)
    return in_maps


def kernel(L0_attn_x, L0_attn_Wq, L0_attn_Wk, L0_attn_Wv, L0_attn_Wo,
           L0_ffn_fast_b, L0_ffn_w_fc, L0_ffn_w_proj, L0_ffn_u, L0_ffn_v,
           L0_ffn_gate):
    in_maps = _stage_inputs(L0_attn_x, L0_attn_Wq, L0_attn_Wk, L0_attn_Wv,
                            L0_attn_Wo, L0_ffn_fast_b, L0_ffn_w_fc,
                            L0_ffn_w_proj, L0_ffn_u, L0_ffn_v, L0_ffn_gate)
    nc = build_nc()
    res = run_bass_kernel_spmd(nc, in_maps, core_ids=list(range(8)))
    out = np.empty((B, T, D), dtype=np.float32)
    for core in range(8):
        beta, i = divmod(core, CPB)
        oc = np.asarray(res.results[core]["outT"]).T
        out[beta, NIN * i:NIN * (i + 1)] = oc[0:NIN]
        out[beta, ACT0 + NACT * i:ACT0 + NACT * (i + 1)] = oc[NIN:TOK]
    return out
